# version 19
# baseline (speedup 1.0000x reference)
import math
import numpy as np

import concourse.bass as bass
import concourse.mybir as mybir
from concourse.bass_utils import run_bass_kernel_spmd

F32 = mybir.dt.float32
AF = mybir.ActivationFunctionType
OP = mybir.AluOpType
AX = mybir.AxisListType

NCORES = 8
S = 4096
DM = 1024
TILE = 512
NT = S // TILE
import os as _os
_TEST_NT = int(_os.environ.get('K_NT', '0'))
MSTOP = int(_os.environ.get('K_MSTOP', '0'))
if _TEST_NT:
    NT = _TEST_NT
MT = 128
DI = 2048
DS = 16
NH = 16
HD = 64
FD = 128
N_SUBLAYERS = int(_os.environ.get('K_NSUB', '4'))

ENGINES = ("tensor", "vector", "scalar", "gpsimd", "sync")
NRING = {"sync": 24, "gpsimd": 8}


class Op:
    __slots__ = ("engine", "fn", "args", "kwargs", "reads", "writes", "kind",
                 "deps", "need_inc", "sem", "val", "inc_amt", "pre_wait",
                 "ptr_reads", "late_keys", "pos")

    def __init__(self, engine, fn, args, kwargs, reads, writes, kind):
        self.engine, self.fn, self.args, self.kwargs = engine, fn, args, kwargs
        self.reads, self.writes, self.kind = reads, writes, kind
        self.deps = ()
        self.need_inc = False
        self.sem = None
        self.val = 0
        self.inc_amt = 1
        self.pre_wait = None
        self.ptr_reads = frozenset()
        self.late_keys = frozenset()
        self.pos = 0


class Buf:

    def __init__(self, nc, name, shape, dtype=F32, gran=None, psum=False):
        if psum:
            self.h = nc.alloc_psum_tensor(name, shape, dtype)
        else:
            self.h = nc.alloc_sbuf_tensor(name, shape, dtype)
        self.name = name
        self.cols = shape[-1]
        self.parts = shape[0]
        self.gran = gran if gran else self.cols

    def keys(self, lo, size):
        g = self.gran
        return frozenset(f"{self.name}:{i}" for i in range(lo // g, (lo + size + g - 1) // g))

    def s(self, lo=0, size=None, p0=0, pn=None):
        if size is None:
            size = self.cols - lo
        if pn is None:
            pn = self.parts - p0
        return (self.h[p0:p0 + pn, lo:lo + size], self.keys(lo, size))

    def ap(self, lo=0, size=None, p0=0, pn=None):
        return self.s(lo, size, p0, pn)[0]


class Ctx:
    def __init__(self, nc):
        self.nc = nc
        self.ops = []
        self.eng_sem = {e: nc.alloc_semaphore(f"es_{e}") for e in ENGINES}
        self.dma_sems = {e: [nc.alloc_semaphore(f"ds_{e}_{i}") for i in range(NRING[e])]
                         for e in NRING}
        self.dma_idx = {e: 0 for e in NRING}
        self.eng_pos = {}
        self.store_ops = []

    def op(self, engine, fn, args, kwargs=None, reads=(), writes=(), kind="compute",
           ptr_reads=(), late=()):
        rd, wr, pr, lt = set(), set(), set(), set()
        for x in reads:
            rd |= x
        for x in writes:
            wr |= x
        for x in ptr_reads:
            rd |= x
            pr |= x
        for x in late:
            lt |= x
        o = Op(engine, fn, list(args), dict(kwargs or {}), frozenset(rd), frozenset(wr), kind)
        if kind == "compute" and args and hasattr(args[0], "shape"):
            shp = args[0].shape
            free = 1
            for d_ in shp[1:]:
                free *= d_
            if free <= 16:
                lt |= wr
        o.ptr_reads = frozenset(pr)
        o.late_keys = frozenset(lt)
        o.pos = self.eng_pos.get(engine, 0)
        self.eng_pos[engine] = o.pos + 1
        self.ops.append(o)
        return o

    def load(self, dst, src_ap, engine="sync"):
        dap, dk = dst
        return self.op(engine, "dma_start", [dap, src_ap], {}, [], [dk], kind="dma")

    def store(self, dst_ap, src, engine="gpsimd"):
        sap, sk = src
        o = self.op(engine, "dma_start", [dst_ap, sap], {}, [sk], [], kind="dma")
        self.store_ops.append(o)
        return o

    def mm(self, out, lhsT, rhs, start=True, stop=True):
        oap, ok = out
        lap, lk = lhsT
        rap, rk = rhs
        return self.op("tensor", "matmul", [oap, lap, rap],
                       dict(start=start, stop=stop), [lk, rk], [ok])

    def tr(self, out, in_, ident):
        oap, ok = out
        iap, ik = in_
        return self.op("tensor", "transpose", [oap, iap, ident[0]], {}, [ik, ident[1]], [ok])

    def cp(self, eng, out, in_):
        oap, ok = out
        iap, ik = in_
        fn = "tensor_copy" if eng in ("vector", "gpsimd") else "copy"
        return self.op(eng, fn, [oap, iap], {}, [ik], [ok])

    def tt(self, eng, out, in0, in1, op):
        oap, ok = out
        return self.op(eng, "tensor_tensor", [oap, in0[0], in1[0]], dict(op=op),
                       [in0[1], in1[1]], [ok])

    def ts(self, eng, out, in0, s1, s2, op0, op1=None):
        oap, ok = out
        reads = [in0[1]]
        s1a = s1
        if isinstance(s1, tuple):
            s1a = s1[0]; reads.append(s1[1])
        s2a = s2
        if isinstance(s2, tuple):
            s2a = s2[0]; reads.append(s2[1])
        if op1 is None:
            return self.op(eng, "tensor_scalar", [oap, in0[0], s1a, None, op0], {},
                           [in0[1]], [ok], ptr_reads=reads[1:])
        return self.op(eng, "tensor_scalar", [oap, in0[0], s1a, s2a, op0, op1], {},
                       [in0[1]], [ok], ptr_reads=reads[1:])

    def ts_accum(self, eng, out, in0, s1, op0, accum_out, seed=0.0):
        oap, ok = out
        aap, akk = accum_out
        reads = [in0[1]]
        s1a = s1
        if isinstance(s1, tuple):
            s1a = s1[0]; reads.append(s1[1])
        return self.op(eng, "tensor_scalar", [oap, in0[0], s1a, seed, op0, OP.add],
                       dict(accum_out=aap), [in0[1]], [ok, akk],
                       ptr_reads=reads[1:], late=[akk])

    def stt(self, eng, out, in0, scalar, in1, op0, op1):
        oap, ok = out
        reads = [in0[1], in1[1]]
        sa = scalar
        if isinstance(scalar, tuple):
            sa = scalar[0]; reads.append(scalar[1])
        return self.op(eng, "scalar_tensor_tensor", [oap, in0[0], sa, in1[0], op0, op1],
                       {}, [in0[1], in1[1]], [ok], ptr_reads=reads[2:])

    def act(self, out, in_, func, bias=None, scale=None, accum_out=None):
        oap, ok = out
        reads = [in_[1]]
        preads = []
        kw = {}
        if bias is not None:
            if isinstance(bias, tuple):
                kw["bias"] = bias[0]; preads.append(bias[1])
            else:
                kw["bias"] = bias
        if scale is not None:
            if isinstance(scale, tuple):
                kw["scale"] = scale[0]; preads.append(scale[1])
            else:
                kw["scale"] = scale
        writes = [ok]
        late = []
        if accum_out is not None:
            kw["accum_out"] = accum_out[0]
            writes.append(accum_out[1])
            late.append(accum_out[1])
        return self.op("scalar", "activation", [oap, in_[0], func], kw, reads, writes,
                       ptr_reads=preads, late=late)

    def scan(self, out, d0, d1, init, op0=OP.mult, op1=OP.add):
        oap, ok = out
        return self.op("vector", "tensor_tensor_scan", [oap, d0[0], d1[0], init[0], op0, op1],
                       {}, [d0[1], d1[1]], [ok], ptr_reads=[init[1]], late=[ok])

    def red(self, out, in_, op=OP.add, fn=None):
        oap, ok = out
        if fn:
            return self.op("vector", fn, [oap, in_[0]], dict(axis=AX.X), [in_[1]], [ok],
                           late=[ok])
        return self.op("vector", "tensor_reduce", [oap, in_[0]], dict(axis=AX.X, op=op),
                       [in_[1]], [ok], late=[ok])

    def memset(self, eng, dst, val):
        dap, dk = dst
        return self.op(eng, "memset", [dap, val], {}, [], [dk])

    def barrier(self):
        last = {}
        for i, o in enumerate(self.ops):
            if o.kind != "nop":
                last[o.engine] = i
        for e in ENGINES:
            o = self.op(e, "barrier_wait", [], {}, [], [], kind="nop")
            o.deps = tuple(last[e2] for e2 in ENGINES if e2 != e and e2 in last)
            for j in o.deps:
                self.ops[j].need_inc = True
        return last

    def resolve(self):
        last_w = {}
        readers = {}
        for i, o in enumerate(self.ops):
            if o.kind == "nop":
                last_w.clear()
                readers.clear()
                continue
            dep = set()
            W = 48
            for k in o.reads:
                j = last_w.get(k)
                if j is None:
                    continue
                w = self.ops[j]
                if w.engine != o.engine:
                    dep.add(j)
                elif (k in o.ptr_reads or k in w.late_keys) and (o.pos - w.pos) < W:
                    dep.add(j)
            for k in o.writes:
                j = last_w.get(k)
                if j is not None and self.ops[j].engine != o.engine:
                    dep.add(j)
                for e2, j in readers.get(k, {}).items():
                    if e2 != o.engine:
                        dep.add(j)
            o.deps = tuple(dep)
            for j in dep:
                self.ops[j].need_inc = True
            for k in o.reads:
                readers.setdefault(k, {})[o.engine] = i
            for k in o.writes:
                last_w[k] = i
                readers[k] = {}

        cnt = {e: 0 for e in ENGINES}
        for o in self.ops:
            if o.kind == "dma":
                e = o.engine
                j = self.dma_idx[e]
                nr = NRING[e]
                o.sem = self.dma_sems[e][j % nr]
                o.val = 16 * (j // nr + 1)
                o.inc_amt = 16
                o.need_inc = True
                if j >= nr:
                    o.pre_wait = (self.dma_sems[e][j % nr], 16 * (j // nr))
                self.dma_idx[e] += 1
            elif o.need_inc and o.kind != "nop":
                cnt[o.engine] += 1
                o.sem = self.eng_sem[o.engine]
                o.val = cnt[o.engine]
                o.inc_amt = 1

    def emit(self):
        self.resolve()
        streams = {e: [] for e in ENGINES}
        for o in self.ops:
            streams[o.engine].append(o)
        ops_by_idx = self.ops
        nc = self.nc
        with nc.Block() as block:
            def emit_engine(eng_obj, eng_name):
                waited = {}
                for o in streams[eng_name]:
                    tgt = {}
                    for j in o.deps:
                        d = ops_by_idx[j]
                        assert d.sem is not None, (o.fn, d.fn)
                        key = id(d.sem)
                        if tgt.get(key, (None, -1))[1] < d.val:
                            tgt[key] = (d.sem, d.val)
                    if o.pre_wait is not None:
                        sem, val = o.pre_wait
                        key = id(sem)
                        if tgt.get(key, (None, -1))[1] < val:
                            tgt[key] = (sem, val)
                    for sem, val in tgt.values():
                        if waited.get(id(sem), -1) < val:
                            eng_obj.wait_ge(sem, val)
                            waited[id(sem)] = val
                    if o.kind == "nop":
                        continue
                    ins = getattr(eng_obj, o.fn)(*o.args, **o.kwargs)
                    if o.need_inc:
                        ins.then_inc(o.sem, o.inc_amt)
                if eng_name == "gpsimd":
                    fin = {}
                    for o in self.store_ops:
                        if fin.get(id(o.sem), (None, -1))[1] < o.val:
                            fin[id(o.sem)] = (o.sem, o.val)
                    for sem, val in fin.values():
                        if waited.get(id(sem), -1) < val:
                            eng_obj.wait_ge(sem, val)

            @block.tensor
            def _(t):
                emit_engine(t, "tensor")

            @block.vector
            def _(v):
                emit_engine(v, "vector")

            @block.scalar
            def _(s):
                emit_engine(s, "scalar")

            @block.gpsimd
            def _(g):
                emit_engine(g, "gpsimd")

            @block.sync
            def _(s):
                emit_engine(s, "sync")



def build_program():
    nc = bass.Bass("TRN2", target_bir_lowering=False, detect_race_conditions=False)

    d = {}
    def din(name, shape):
        d[name] = nc.dram_tensor(name, list(shape), F32, kind="ExternalInput")
        return d[name]

    x_d = din("x", (S, DM))
    wq_d = din("wq", (DM, DM)); wk_d = din("wk", (DM, DM))
    wv_d = din("wv", (DM, DM)); wo_d = din("wo", (DM, DM))
    rf2_d = din("rf2", (128, 128))
    gw_d = din("gw", (DM, 8))
    w0T_d = din("w0T", (DM, DM)); w1T_d = din("w1T", (DM, DM))
    b0_d = din("b0r", (1, DM)); b1_d = din("b1r", (1, DM))
    normw_d = din("normw", (128, 8))
    inpT_d = din("inpT", (DM, 2 * DI))
    convw_d = din("convw", (128, 64)); convb_d = din("convb", (128, 16))
    xpT_d = din("xpT", (DI, 96))
    dtpT_d = din("dtpT", (64, DI)); dtb_d = din("dtb", (128, 16))
    alog_d = din("alog", (128, 256)); dvec_d = din("dvec", (128, 16))
    opT_d = din("opT", (DI, DM))
    ident_d = din("ident", (128, 128))
    ones16_d = din("ones16", (16, 128))
    shp_d = din("shp", (128, 1)); b1v_d = din("b1v", (128, 1)); b2v_d = din("b2v", (128, 1))
    epsv_d = din("epsv", (128, 2))
    out_d = nc.dram_tensor("out", [S, DM], F32, kind="ExternalOutput")
    DBG = int(_os.environ.get('K_DBGOUT', '0'))
    dbg_d = {}
    if DBG:
        for nm in ("dKVZ", "dC", "dE", "dF", "dD", "dkp", "dvtm", "dkptm", "dKVT", "dKsum",
                   "pE", "pF", "pC", "pkp", "pkptm", "pvtm", "qC", "qln"):
            dbg_d[nm] = nc.dram_tensor(nm, [128, 4096], F32, kind="ExternalOutput")

    A = Buf(nc, "slotA", [128, 4096], gran=256)
    B = Buf(nc, "slotB", [128, 4096], gran=256)
    C = Buf(nc, "slotC", [128, 4096], gran=256)
    Dd = Buf(nc, "slotD", [128, 4096], gran=256)
    E = Buf(nc, "slotE", [128, 4096], gran=256)
    Fb = Buf(nc, "slotF", [128, 4096], gran=256)
    G = Buf(nc, "slotG", [128, 4096], gran=256)
    H = Buf(nc, "slotH", [128, 4096], gran=256)

    NR_W = 5
    WR = Buf(nc, "wring", [128, NR_W * 1024], gran=1024)

    ident = Buf(nc, "ident_sb", [128, 128])
    rf2 = Buf(nc, "rf2_sb", [128, 128])
    ones16 = Buf(nc, "ones16_sb", [16, 128])
    onescol = Buf(nc, "onescol", [128, 1])
    gwS = Buf(nc, "gw_sb", [128, 64], gran=8)
    b0r = Buf(nc, "b0r_sb", [1, DM])
    b1r = Buf(nc, "b1r_sb", [1, DM])
    normw = Buf(nc, "normw_sb", [128, 8], gran=1)
    convw = Buf(nc, "convw_sb", [128, 64], gran=4)
    convb = Buf(nc, "convb_sb", [128, 16], gran=1)
    dtb = Buf(nc, "dtb_sb", [128, 16], gran=1)
    dvec = Buf(nc, "dvec_sb", [128, 16], gran=1)
    Asb = Buf(nc, "A_sb", [128, 256], gran=16)
    dtpT = Buf(nc, "dtpT_sb", [64, DI], gran=128)
    xpT = Buf(nc, "xpT_sb", [128, 16 * 96], gran=96)
    shp = Buf(nc, "shp_sb", [128, 1])
    b1v = Buf(nc, "b1v_sb", [128, 1])
    b2v = Buf(nc, "b2v_sb", [128, 1])
    KVZ = Buf(nc, "KVZ_sb", [128, NH * 65], gran=65)
    lnv = Buf(nc, "lnv", [128, 24], gran=1)
    kp = Buf(nc, "kp", [128, 512])
    sg1 = Buf(nc, "sg1", [128, 1024], gran=512)
    sg2 = Buf(nc, "sg2", [128, 1024], gran=512)
    kptm = Buf(nc, "kptm", [128, 512], gran=128)
    vtm = Buf(nc, "vtm", [128, 520], gran=65)
    esc = Buf(nc, "esc", [128, 512], gran=128)
    gl = Buf(nc, "gl", [128, 8])
    gl0 = Buf(nc, "gl0", [128, 8])
    gmk = Buf(nc, "gmk", [128, 8])
    ge = Buf(nc, "ge", [128, 8])
    gp = Buf(nc, "gp", [128, 2])
    gv = Buf(nc, "gv", [128, 16], gran=1)
    xdbl = Buf(nc, "xdbl", [128, 192], gran=96)
    dlrfm = Buf(nc, "dlrfm", [64, 256])
    BCfm = Buf(nc, "BCfm", [16, 512], gran=256)
    convs = Buf(nc, "convs", [128, 384], gran=128)
    tmp6 = Buf(nc, "tmp6", [128, 6])
    tails = Buf(nc, "tails", [128, 48], gran=3)
    dAb = Buf(nc, "dAb", [128, 1024], gran=128)
    yv = Buf(nc, "yv", [128, 256])
    state = Buf(nc, "state", [128, 256], gran=16)
    zsc = Buf(nc, "zsc", [128, 128], gran=1)
    brow = Buf(nc, "brow", [1, 512], gran=256)
    epsv = Buf(nc, "epsv_sb", [128, 2], gran=1)

    PS = [Buf(nc, f"ps{i}", [128, 512], psum=True) for i in range(8)]

    ctx = Ctx(nc)
    psum_i = [0]

    def nb():
        b = PS[psum_i[0] % 8]
        psum_i[0] += 1
        return b

    wslot = [0]
    def fetch(dram, r0, c0, ncols=1024):
        sl = wslot[0] % NR_W
        wslot[0] += 1
        dst = WR.s(sl * 1024, ncols)
        ctx.load(dst, dram[r0:r0 + 128, c0:c0 + ncols])
        return dst

    ctx.load(ident.s(), ident_d[:, :])
    ctx.load(rf2.s(), rf2_d[:, :])
    ctx.load(ones16.s(), ones16_d[:, :])
    for ki in range(8):
        ctx.load(gwS.s(ki * 8, 8), gw_d[ki * 128:(ki + 1) * 128, :])
    ctx.load(b0r.s(), b0_d[:, :])
    ctx.load(b1r.s(), b1_d[:, :])
    ctx.load(normw.s(), normw_d[:, :])
    ctx.load(convw.s(), convw_d[:, :])
    ctx.load(convb.s(), convb_d[:, :])
    ctx.load(dtb.s(), dtb_d[:, :])
    ctx.load(dvec.s(), dvec_d[:, :])
    ctx.load(Asb.s(), alog_d[:, :])
    ctx.load(dtpT.s(), dtpT_d[:, :])
    for kj in range(16):
        ctx.load(xpT.s(kj * 96, 96), xpT_d[kj * 128:(kj + 1) * 128, :])
    ctx.load(shp.s(), shp_d[:, :])
    ctx.load(b1v.s(), b1v_d[:, :])
    ctx.load(b2v.s(), b2v_d[:, :])
    ctx.load(epsv.s(), epsv_d[:, :])
    ctx.memset("gpsimd", onescol.s(), 1.0)
    ctx.op("gpsimd", "memset", [bass.AP(vtm.h, 64, [[vtm.cols, 128], [130, 4], [65, 2]]), 1.0],
           {}, [], [vtm.keys(0, 520)])
    ctx.memset("gpsimd", G.s(0, 2048), 0.0)
    ctx.memset("gpsimd", state.s(), 0.0)
    ctx.memset("gpsimd", lnv.s(), 0.0)
    ctx.memset("gpsimd", zsc.s(), 0.0)
    ctx.memset("gpsimd", gv.s(), 0.0)
    ctx.memset("gpsimd", tails.s(), 0.0)
    ctx.act(Asb.s(), Asb.s(), AF.Exp)
    ctx.ts("vector", Asb.s(), Asb.s(), -1.0, None, OP.mult)


    def load_x_tile(T):
        for t_s in range(4):
            ctx.load(A.s(t_s * 1024, 1024),
                     x_d[T * 512 + t_s * 128: T * 512 + (t_s + 1) * 128, :])

    def layer_norm(src, t_s, ln0):
        c_s, c_m, c_q, c_r = ln0, ln0 + 1, ln0 + 2, ln0 + 3
        ctx.red(lnv.s(c_s, 1), src.s(t_s * 1024, 1024), fn="reduce_sum")
        ctx.ts("vector", lnv.s(c_m, 1), lnv.s(c_s, 1), -1.0 / DM, None, OP.mult)
        ctx.ts("vector", C.s(t_s * 1024, 1024), src.s(t_s * 1024, 1024),
               lnv.s(c_m, 1), None, OP.add)
        ctx.act(Dd.s(0, 1024), C.s(t_s * 1024, 1024), AF.Square, accum_out=lnv.s(c_q, 1))
        ctx.act(lnv.s(c_r, 1), lnv.s(c_q, 1), AF.Ln, bias=epsv.s(0, 1), scale=1.0 / DM)
        ctx.act(lnv.s(c_r, 1), lnv.s(c_r, 1), AF.Exp, scale=-0.5)
        ctx.ts("vector", C.s(t_s * 1024, 1024), C.s(t_s * 1024, 1024),
               lnv.s(c_r, 1), None, OP.mult)

    def transpose_tile(src, dst):
        for dc in range(8):
            bank = nb()
            for t_s in range(4):
                ctx.tr((bank.ap(t_s * 128, 128), bank.keys(0, 512)),
                       src.s(t_s * 1024 + dc * 128, 128), ident.s())
            ctx.cp("vector", dst.s(dc * 512, 512), (bank.ap(0, 512), bank.keys(0, 512)))

    def proj1024(w_dram, src_fm, dst_fm, eng_evac="scalar"):
        banks = [nb() for _ in range(8)]
        for ki in range(8):
            wch = fetch(w_dram, ki * 128, 0, 1024)
            for mo in range(8):
                lhsT = (wch[0][:, mo * 128:(mo + 1) * 128], wch[1])
                ctx.mm((banks[mo].ap(), banks[mo].keys(0, 512)), lhsT,
                       src_fm.s(ki * 512, 512), start=(ki == 0), stop=(ki == 7))
        for mo in range(8):
            ctx.cp(eng_evac, dst_fm.s(mo * 512, 512),
                   (banks[mo].ap(), banks[mo].keys(0, 512)))

    def phi_front(src_fm, h):
        mo, poff, par = h // 2, (h % 2) * 64, h % 2
        bank = nb()
        pref = (bank.ap(0, 512), bank.keys(0, 512))
        ctx.mm(pref, rf2.s(0, 128, p0=poff, pn=64),
               src_fm.s(mo * 512, 512, p0=poff, pn=64))
        ctx.act(sg1.s(par * 512, 512), pref, AF.Sign, bias=b1v.s())
        ctx.act(sg2.s(par * 512, 512), pref, AF.Sign, bias=b2v.s(), scale=-1.0)
        return pref

    def phi_back(pref, h, dst):
        par = h % 2
        g1, g2 = sg1.s(par * 512, 512), sg2.s(par * 512, 512)
        ctx.tt("vector", g2, g2, g1, OP.subtract)
        ctx.ts("vector", g2, g2, float(np.pi), None, OP.mult)
        ctx.tt("vector", dst.s(), g2, pref, OP.add)
        ctx.ts("vector", dst.s(0, 512, p0=0, pn=64), dst.s(0, 512, p0=0, pn=64),
               float(np.pi) / 2, None, OP.add)
        CL = float(np.float32(3.1415925))
        ctx.ts("vector", dst.s(), dst.s(), -CL, CL, OP.max, OP.min)
        ctx.act(dst.s(), dst.s(), AF.Sin)

    def ffn_block():
        transpose_tile(C, Dd)
        for t_s in range(4):
            gbank = nb()
            gref = (gbank.ap(0, 8), gbank.keys(0, 512))
            for ki in range(8):
                ctx.mm(gref, Dd.s(ki * 512 + t_s * 128, 128), gwS.s(ki * 8, 8),
                       start=(ki == 0), stop=(ki == 7))
            ctx.cp("vector", gl0.s(), gref)
            ctx.red(gv.s(0, 1), gref, fn="reduce_max")
            ctx.ts("vector", gl.s(), gref, gv.s(0, 1), None, OP.subtract)
            ctx.act(ge.s(), gl.s(), AF.Exp)
            ctx.red(gv.s(1, 1), ge.s(), fn="reduce_sum")
            ctx.op("vector", "reciprocal", [gv.ap(2, 1), gv.ap(1, 1)], {},
                   [gv.keys(1, 1)], [gv.keys(2, 1)], late=[gv.keys(2, 1)])
            ctx.ts("vector", gp.s(), ge.s(0, 2), gv.s(2, 1), None, OP.mult)
            ctx.tt("vector", gv.s(3, 1), gl0.s(0, 1), gv.s(0, 1), OP.is_ge)
            ctx.ts_accum("vector", gmk.s(), gl0.s(), gl0.s(1, 1),
                         OP.is_gt, gv.s(4, 1))
            ctx.tt("vector", gv.s(5, 1), gl0.s(0, 1), gl0.s(1, 1), OP.is_equal)
            ctx.tt("vector", gv.s(6, 1), gv.s(4, 1), gv.s(5, 1), OP.add)
            ctx.op("vector", "tensor_single_scalar",
                   [gv.ap(7, 1), gv.ap(6, 1), 1.0, OP.is_equal], {},
                   [gv.keys(6, 1)], [gv.keys(7, 1)])
            ctx.tt("vector", gv.s(8 + t_s, 1), gp.s(0, 1), gv.s(3, 1), OP.mult)
            ctx.tt("vector", gv.s(12 + t_s, 1), gp.s(1, 1), gv.s(7, 1), OP.mult)
        for ex, (w_dram, brow, ccol0) in enumerate([(w0T_d, b0r, 8), (w1T_d, b1r, 12)]):
            for jb in range(2):
                ebanks = [nb() for _ in range(4)]
                for ki in range(8):
                    wch = fetch(w_dram, ki * 128, jb * 512, 512)
                    for t_s in range(4):
                        ctx.mm((ebanks[t_s].ap(), ebanks[t_s].keys(0, 512)),
                               Dd.s(ki * 512 + t_s * 128, 128), wch,
                               start=(ki == 0), stop=False)
                for t_s in range(4):
                    ctx.mm((ebanks[t_s].ap(), ebanks[t_s].keys(0, 512)),
                           ones16.s(0, 128, p0=0, pn=1),
                           brow.s(jb * 512, 512), start=False, stop=True)
                    eslc = C.s(t_s * 1024 + jb * 512, 512)
                    ctx.act(eslc, (ebanks[t_s].ap(), ebanks[t_s].keys(0, 512)),
                            AF.Silu)
                    ctx.stt("vector", B.s(t_s * 1024 + jb * 512, 512), eslc,
                            gv.s(ccol0 + t_s, 1), B.s(t_s * 1024 + jb * 512, 512),
                            OP.mult, OP.add)

    for T in range(NT):
        load_x_tile(T)
        for t_s in range(4):
            layer_norm(A, t_s, 4 * t_s)
        if DBG and T == 0:
            ctx.store(dbg_d["qC"][:, :], C.s())
            ctx.store(dbg_d["qln"][:, 0:16], lnv.s())
        transpose_tile(C, Dd)
        proj1024(wk_d, Dd, E)
        proj1024(wv_d, Dd, Fb)
        pref_next = phi_front(E, 0)
        for mo in range(8):
            vb = nb()
            for t_s in range(4):
                ctx.tr((vb.ap(t_s * 128, 128), vb.keys(0, 512)),
                       Fb.s(mo * 512 + t_s * 128, 128), ident.s())
            ctx.op("vector", "tensor_copy",
                   [bass.AP(vtm.h, 0, [[vtm.cols, 128], [130, 4], [65, 2], [1, 64]]),
                    bass.AP(vb.h, 0, [[vb.cols, 128], [128, 4], [64, 2], [1, 64]])],
                   {}, [vb.keys(0, 512)], [vtm.keys(0, 520)])
            for hh in range(2):
                h = mo * 2 + hh
                pref = pref_next
                if h + 1 < NH:
                    pref_next = phi_front(E, h + 1)
                phi_back(pref, h, kp)
                kb = nb()
                for t_s in range(4):
                    ctx.tr((kb.ap(t_s * 128, 128), kb.keys(0, 512)),
                           kp.s(t_s * 128, 128), ident.s())
                ctx.cp("vector", kptm.s(0, 512), (kb.ap(0, 512), kb.keys(0, 512)))
                kvb = nb()
                kvref = (kvb.ap(0, 65), kvb.keys(0, 512))
                for t_s in range(4):
                    ctx.mm(kvref, kptm.s(t_s * 128, 128),
                           vtm.s(t_s * 130 + hh * 65, 65),
                           start=(t_s == 0), stop=(t_s == 3))
                ctx.tt("vector", G.s(h * 65, 65), kvref, G.s(h * 65, 65), OP.add)

    ctx.barrier()
    if DBG:
        ctx.store(dbg_d["pE"][:, :], E.s())
        ctx.store(dbg_d["pF"][:, :], Fb.s())
        ctx.store(dbg_d["pC"][:, :], C.s())
        ctx.store(dbg_d["pkp"][:, 0:512], kp.s())
        ctx.store(dbg_d["pkptm"][:, 0:512], kptm.s())
        ctx.store(dbg_d["pvtm"][:, 0:512], vtm.s())
        ctx.barrier()
    ctx.cp("vector", KVZ.s(0, NH * 65), G.s(0, NH * 65))
    ctx.barrier()

    for T in range(NT):
        load_x_tile(T)
        for t_s in range(4):
            layer_norm(A, t_s, 4 * t_s)
        transpose_tile(C, Dd)
        proj1024(wq_d, Dd, E)
        pref_next = phi_front(E, 0)
        for h in range(NH):
            pref = pref_next
            if h + 1 < NH:
                pref_next = phi_front(E, h + 1)
            phi_back(pref, h, kp)
            hq = (h % 4) * 16
            qbs = [nb() for _ in range(4)]
            for t_s in range(4):
                qref = (qbs[t_s].ap(0, 65), qbs[t_s].keys(0, 512))
                ctx.mm(qref, kp.s(t_s * 128, 128), KVZ.s(h * 65, 65))
            for t_s in range(4):
                ctx.ts("vector", zsc.s(hq + t_s, 1),
                       (qbs[t_s].ap(64, 1), qbs[t_s].keys(0, 512)),
                       float(128e-6), None, OP.max)
            ctx.op("vector", "reciprocal", [zsc.ap(hq + 4, 4), zsc.ap(hq, 4)], {},
                   [zsc.keys(hq, 4)], [zsc.keys(hq + 4, 4)], late=[zsc.keys(hq + 4, 4)])
            for t_s in range(4):
                ctx.ts("vector", Fb.s(t_s * 1024 + h * 64, 64),
                       (qbs[t_s].ap(0, 64), qbs[t_s].keys(0, 512)),
                       zsc.s(hq + 4 + t_s, 1), None, OP.mult)
        transpose_tile(Fb, Dd)
        banks = [nb() for _ in range(8)]
        for ki in range(8):
            wch = fetch(wo_d, ki * 128, 0, 1024)
            for i8, (t_s, ob) in enumerate([(a, b) for a in range(4) for b in range(2)]):
                lhsT = Dd.s(ki * 512 + t_s * 128, 128)
                rhs = (wch[0][:, ob * 512:(ob + 1) * 512], wch[1])
                ctx.mm((banks[i8].ap(), banks[i8].keys(0, 512)), lhsT, rhs,
                       start=(ki == 0), stop=(ki == 7))
        for i8, (t_s, ob) in enumerate([(a, b) for a in range(4) for b in range(2)]):
            ctx.tt("vector", B.s(t_s * 1024 + ob * 512, 512),
                   (banks[i8].ap(), banks[i8].keys(0, 512)),
                   A.s(t_s * 1024 + ob * 512, 512), OP.add)

        if N_SUBLAYERS >= 2:
            for t_s in range(4):
                layer_norm(B, t_s, 4 * t_s)
            ffn_block()

        if N_SUBLAYERS >= 3:
            for t_s in range(4):
                layer_norm(B, t_s, 4 * t_s)
                ctx.tt("vector", B.s(t_s * 1024, 1024), B.s(t_s * 1024, 1024),
                       C.s(t_s * 1024, 1024), OP.add)
                ctx.act(Dd.s(0, 1024), C.s(t_s * 1024, 1024), AF.Square,
                        accum_out=lnv.s(16, 1))
                ctx.act(lnv.s(17, 1), lnv.s(16, 1), AF.Ln, bias=epsv.s(0, 1), scale=1.0 / DM)
                ctx.act(lnv.s(17, 1), lnv.s(17, 1), AF.Exp, scale=-0.5)
                ctx.ts("vector", C.s(t_s * 1024, 1024), C.s(t_s * 1024, 1024),
                       lnv.s(17, 1), None, OP.mult)
            transpose_tile(C, Dd)
            for dc8 in range(8):
                ctx.ts("vector", Dd.s(dc8 * 512, 512), Dd.s(dc8 * 512, 512),
                       normw.s(dc8, 1), None, OP.mult)
            def m_front(mm_i):
                    pp = mm_i % 2
                    hb = pp * 2048
                    for jg in range(4):
                        banks = [nb() for _ in range(8)]
                        for ki in range(8):
                            wch = fetch(inpT_d, ki * 128, jg * 1024, 1024)
                            for jcl in range(8):
                                lhsT = (wch[0][:, jcl * 128:(jcl + 1) * 128], wch[1])
                                ctx.mm((banks[jcl].ap(0, 128), banks[jcl].keys(0, 512)),
                                       lhsT, Dd.s(ki * 512 + mm_i * 128, 128),
                                       start=(ki == 0), stop=(ki == 7))
                        for jcl in range(8):
                            jc = jg * 8 + jcl
                            dstbuf, dcc = (E, jc) if jc < 16 else (Fb, jc - 16)
                            ctx.cp("scalar", dstbuf.s(hb + dcc * 128, 128),
                                   (banks[jcl].ap(0, 128), banks[jcl].keys(0, 512)))
                    for dc in range(16):
                        eb0 = hb + dc * 128
                        ctx.cp("vector", (tmp6.ap(0, 3), tmp6.keys(0, 6)), tails.s(dc * 3, 3))
                        ctx.cp("vector", (tmp6.ap(3, 3), tmp6.keys(0, 6)), E.s(eb0, 3))
                        ctx.cp("vector", tails.s(dc * 3, 3), E.s(eb0 + 125, 3))
                        ctx.ts("vector", convs.s(3, 125), E.s(eb0 + 0, 125),
                               convw.s(dc * 4 + 0, 1), convb.s(dc, 1), OP.mult, OP.add)
                        for k in range(1, 4):
                            ctx.stt("vector", convs.s(3, 125), E.s(eb0 + k, 125),
                                    convw.s(dc * 4 + k, 1), convs.s(3, 125), OP.mult, OP.add)
                        ctx.ts("vector", convs.s(0, 3), (tmp6.ap(0, 3), tmp6.keys(0, 6)),
                               convw.s(dc * 4 + 0, 1), convb.s(dc, 1), OP.mult, OP.add)
                        for k in range(1, 4):
                            ctx.stt("vector", convs.s(0, 3), (tmp6.ap(k, 3), tmp6.keys(0, 6)),
                                    convw.s(dc * 4 + k, 1), convs.s(0, 3), OP.mult, OP.add)
                        ctx.cp("vector", E.s(eb0, 128), convs.s(0, 128))
                    ctx.act(E.s(hb, 2048), E.s(hb, 2048), AF.Silu)
                    if mm_i >= 1:
                        po = (1 - pp) * 2048
                        ctx.act(Fb.s(po, 2048), Fb.s(po, 2048), AF.Silu)
                    xb = nb()
                    xref = (xb.ap(0, 96), xb.keys(0, 512))
                    for kj in range(16):
                        ctx.mm(xref, E.s(hb + kj * 128, 128), xpT.s(kj * 96, 96),
                               start=(kj == 0), stop=(kj == 15))
                    ctx.cp("vector", xdbl.s(pp * 96, 96), xref)
                    tb = nb()
                    ctx.tr((tb.ap(0, 128, p0=0, pn=64), tb.keys(0, 512)),
                           xdbl.s(pp * 96, 64), ident.s())
                    ctx.cp("vector", dlrfm.s(pp * 128, 128, p0=0, pn=64),
                           (tb.ap(0, 128, p0=0, pn=64), tb.keys(0, 512)))
                    tb2 = nb()
                    ctx.tr((tb2.ap(0, 128, p0=0, pn=16), tb2.keys(0, 512)),
                           xdbl.s(pp * 96 + 64, 16), ident.s())
                    ctx.cp("vector", BCfm.s(pp * 256, 128, p0=0, pn=16),
                           (tb2.ap(0, 128, p0=0, pn=16), tb2.keys(0, 512)))
                    tb3 = nb()
                    ctx.tr((tb3.ap(0, 128, p0=0, pn=16), tb3.keys(0, 512)),
                           xdbl.s(pp * 96 + 80, 16), ident.s())
                    ctx.cp("vector", BCfm.s(pp * 256 + 128, 128, p0=0, pn=16),
                           (tb3.ap(0, 128, p0=0, pn=16), tb3.keys(0, 512)))
                    for dc in range(16):
                        db = nb()
                        dref = (db.ap(0, 128), db.keys(0, 512))
                        ctx.mm(dref, dtpT.s(dc * 128, 128), dlrfm.s(pp * 128, 128))
                        ctx.act(esc.s(pp * 128, 128), dref, AF.Exp, bias=dtb.s(dc, 1))
                        ctx.act(G.s(hb + dc * 128, 128), esc.s(pp * 128, 128), AF.Ln,
                                bias=epsv.s(1, 1))
                    for n in range(DS):
                        xb1 = nb()
                        ctx.mm((xb1.ap(0, 256, p0=0, pn=1), xb1.keys(0, 512)),
                               ident.s(n, 1, p0=0, pn=16), BCfm.s(pp * 256, 256))
                        ctx.cp("vector", brow.s(pp * 256, 256),
                               (xb1.ap(0, 256, p0=0, pn=1), xb1.keys(0, 512)))
                        bb = nb()
                        ctx.mm((bb.ap(0, 256), bb.keys(0, 512)),
                               ones16.s(0, 128, p0=0, pn=1), brow.s(pp * 256, 256))
                        ctx.cp("scalar", C.s(hb + n * 128, 128), (bb.ap(0, 128), bb.keys(0, 512)))
                        ctx.cp("scalar", A.s(hb + n * 128, 128), (bb.ap(128, 128), bb.keys(0, 512)))

            def m_back(mm_i):
                pp = mm_i % 2
                hb = pp * 2048
                if mm_i == 3:
                    ctx.act(Fb.s(hb, 2048), Fb.s(hb, 2048), AF.Silu)
                def slot(i):
                    return (i % 4) * 1024

                def du_part(dc):
                    ctx.tt("gpsimd", convs.s(128 + (dc % 2) * 128, 128),
                           G.s(hb + dc * 128, 128), E.s(hb + dc * 128, 128), OP.mult)

                def dbu_half(i):
                    dc, half = divmod(i, 2)
                    du_rep = bass.AP(convs.h, 128 + (dc % 2) * 128,
                                     [[convs.cols, 128], [0, 8], [1, 128]])
                    ctx.op("gpsimd", "tensor_tensor",
                           [H.h[:, slot(i): slot(i) + 1024]
                            .rearrange("p (a b) -> p a b", a=8),
                            du_rep,
                            C.h[:, hb + half * 1024: hb + half * 1024 + 1024]
                            .rearrange("p (a b) -> p a b", a=8),
                            OP.mult],
                           {}, [convs.keys(128 + (dc % 2) * 128, 128),
                                C.keys(hb + half * 1024, 1024)],
                           [H.keys(slot(i), 1024)])

                def scans_half(i):
                    dc, half = divmod(i, 2)
                    eb0 = hb + dc * 128
                    for j in range(8):
                        n = half * 8 + j
                        ctx.act(dAb.s(j * 128, 128), G.s(eb0, 128), AF.Exp,
                                scale=Asb.s(dc * 16 + n, 1))
                    for j in range(8):
                        n = half * 8 + j
                        hcol = slot(i) + j * 128
                        ctx.op("vector", "tensor_tensor_scan",
                               [H.h[:, hcol: hcol + 128], dAb.ap(j * 128, 128),
                                H.h[:, hcol: hcol + 128],
                                state.ap(dc * 16 + n, 1), OP.mult, OP.add], {},
                               [dAb.keys(j * 128, 128), H.keys(hcol, 128)],
                               [H.keys(hcol, 128)],
                               ptr_reads=[state.keys(dc * 16 + n, 1)],
                               late=[H.keys(hcol, 128)])
                    ctx.op("vector", "tensor_copy",
                           [bass.AP(state.h, dc * 16 + half * 8, [[state.cols, 128], [1, 8]]),
                            bass.AP(H.h, slot(i) + 127, [[H.cols, 128], [128, 8]])],
                           {}, [H.keys(slot(i), 1024)],
                           [state.keys(dc * 16 + half * 8, 8)])

                def sC_half(i):
                    dc, half = divmod(i, 2)
                    ctx.tt("gpsimd", H.s(slot(i), 1024), H.s(slot(i), 1024),
                           A.s(hb + half * 1024, 1024), OP.mult)

                def tree_gp(i):
                    b0 = slot(i)
                    ctx.tt("gpsimd", H.s(b0, 512), H.s(b0, 512), H.s(b0 + 512, 512), OP.add)
                    ctx.tt("gpsimd", H.s(b0, 256), H.s(b0, 256), H.s(b0 + 256, 256), OP.add)

                def tree_v(i):
                    dc, half = divmod(i, 2)
                    b0 = slot(i)
                    ctx.tt("vector", yv.s(half * 128, 128), H.s(b0, 128),
                           H.s(b0 + 128, 128), OP.add)

                def y_tail(dc):
                    eb0 = hb + dc * 128
                    ctx.tt("vector", yv.s(0, 128), yv.s(0, 128), yv.s(128, 128), OP.add)
                    ctx.stt("vector", yv.s(0, 128), E.s(eb0, 128),
                            dvec.s(dc, 1), yv.s(0, 128), OP.mult, OP.add)
                    ctx.tt("vector", E.s(eb0, 128), yv.s(0, 128),
                           Fb.s(eb0, 128), OP.mult)

                du_part(0)
                dbu_half(0)
                dbu_half(1)
                for i in range(32):
                    if i >= 2:
                        tree_gp(i - 2)
                        tree_v(i - 2)
                        if (i - 2) % 2 == 1:
                            y_tail((i - 2) // 2)
                    scans_half(i)
                    if i + 2 < 32:
                        if (i + 2) % 2 == 0:
                            du_part((i + 2) // 2)
                        dbu_half(i + 2)
                    if i >= 1:
                        sC_half(i - 1)
                sC_half(31)
                for i in (30, 31):
                    tree_gp(i)
                    tree_v(i)
                y_tail(15)
                banks2 = [nb() for _ in range(2)]
                for kj in range(16):
                    wch = fetch(opT_d, kj * 128, 0, 1024)
                    for ob in range(2):
                        rhs = (wch[0][:, ob * 512:(ob + 1) * 512], wch[1])
                        ctx.mm((banks2[ob].ap(), banks2[ob].keys(0, 512)),
                               E.s(hb + kj * 128, 128), rhs,
                               start=(kj == 0), stop=(kj == 15))
                for ob in range(2):
                    col = mm_i * 1024 + ob * 512
                    ctx.tt("vector", B.s(col, 512),
                           (banks2[ob].ap(), banks2[ob].keys(0, 512)),
                           B.s(col, 512), OP.add)


            for k in range(5):
                if k < 4:
                    m_front(k)
                if k >= 1:
                    m_back(k - 1)
        if N_SUBLAYERS >= 4:
            for t_s in range(4):
                layer_norm(B, t_s, 4 * t_s)
            ffn_block()

        for t_s in range(4):
            ctx.store(out_d[T * 512 + t_s * 128: T * 512 + (t_s + 1) * 128, :],
                      B.s(t_s * 1024, 1024))
    if DBG:
        ctx.barrier()
        ctx.store(dbg_d["dKVZ"][:, 0:KVZ.cols], KVZ.s())
        ctx.store(dbg_d["dC"][:, :], C.s())
        ctx.store(dbg_d["dE"][:, :], E.s())
        ctx.store(dbg_d["dF"][:, :], Fb.s())
        ctx.store(dbg_d["dD"][:, :], Dd.s())
        ctx.store(dbg_d["dkp"][:, 0:512], kp.s())
        ctx.store(dbg_d["dvtm"][:, 0:512], vtm.s())
        ctx.store(dbg_d["dkptm"][:, 0:512], kptm.s())
        ctx.store(dbg_d["dKVT"][:, 0:2048], G.s(0, 2048))
        ctx.store(dbg_d["dKsum"][:, 0:16], Ksum.s())

    ctx.emit()
    return nc



_CACHE = {}

def host_shared(L0_attn_a_Wq, L0_attn_a_Wk, L0_attn_a_Wv, L0_attn_a_Wo,
                L0_attn_a_rf, L0_attn_b_norm_w, L0_attn_b_in_proj_w, L0_attn_b_conv_w,
                L0_attn_b_conv_b, L0_attn_b_x_proj_w, L0_attn_b_dt_proj_w,
                L0_attn_b_dt_proj_b, L0_attn_b_A_log, L0_attn_b_D, L0_attn_b_out_proj_w,
                L0_ffn_gate_w, L0_ffn_w0, L0_ffn_b0, L0_ffn_w1, L0_ffn_b1, **_ignore):
    f = np.float32
    cont = np.ascontiguousarray

    shp_v = np.zeros((128, 1), f); shp_v[0:64] = np.pi / 2
    pi = np.pi
    shared = {
        "wq": cont(L0_attn_a_Wq, dtype=f), "wk": cont(L0_attn_a_Wk, dtype=f),
        "wv": cont(L0_attn_a_Wv, dtype=f), "wo": cont(L0_attn_a_Wo, dtype=f),
        "rf2": cont(np.concatenate([L0_attn_a_rf, L0_attn_a_rf], axis=0), dtype=f),
        "gw": cont(L0_ffn_gate_w, dtype=f),
        "w0T": cont(L0_ffn_w0.T, dtype=f), "w1T": cont(L0_ffn_w1.T, dtype=f),
        "b0r": cont(L0_ffn_b0.reshape(1, DM), dtype=f),
        "b1r": cont(L0_ffn_b1.reshape(1, DM), dtype=f),
        "normw": cont(L0_attn_b_norm_w.reshape(8, 128).T, dtype=f),
        "inpT": cont(L0_attn_b_in_proj_w.T, dtype=f),
        "convw": cont(L0_attn_b_conv_w[:, 0, :].reshape(16, 128, 4)
                      .transpose(1, 0, 2).reshape(128, 64), dtype=f),
        "convb": cont(L0_attn_b_conv_b.reshape(16, 128).T, dtype=f),
        "xpT": cont(L0_attn_b_x_proj_w.T, dtype=f),
        "dtpT": cont(L0_attn_b_dt_proj_w.T, dtype=f),
        "dtb": cont(L0_attn_b_dt_proj_b.reshape(16, 128).T, dtype=f),
        "alog": cont(np.asarray(L0_attn_b_A_log).reshape(16, 128, 16)
                     .transpose(1, 0, 2).reshape(128, 256), dtype=f),
        "dvec": cont(L0_attn_b_D.reshape(16, 128).T, dtype=f),
        "opT": cont(L0_attn_b_out_proj_w.T, dtype=f),
        "ident": np.eye(128, dtype=f),
        "ones16": np.ones((16, 128), f),
        "shp": shp_v,
        "b1v": cont(shp_v - pi, dtype=f),
        "b2v": cont(-shp_v - pi, dtype=f),
        "epsv": cont(np.broadcast_to(np.array([1e-5, 1.0], f), (128, 2)), dtype=f),
    }
    return shared


def kernel(L0_attn_a_x, **kw):
    shared = host_shared(**kw)
    X = np.ascontiguousarray(L0_attn_a_x, dtype=np.float32)
    in_maps = [dict(shared, x=np.ascontiguousarray(X[b])) for b in range(NCORES)]

    if "nc" not in _CACHE:
        _CACHE["nc"] = build_program()
    nc = _CACHE["nc"]
    res = run_bass_kernel_spmd(nc, in_maps, core_ids=list(range(NCORES)))
    out = np.stack([np.asarray(res.results[b]["out"]) for b in range(NCORES)], axis=0)
    return out.astype(np.float32)


# version 20
# speedup vs baseline: 1.1091x; 1.1091x over previous
import math
import numpy as np

import concourse.bass as bass
import concourse.mybir as mybir
from concourse.bass_utils import run_bass_kernel_spmd

F32 = mybir.dt.float32
AF = mybir.ActivationFunctionType
OP = mybir.AluOpType
AX = mybir.AxisListType

NCORES = 8
S = 4096
DM = 1024
TILE = 512
NT = S // TILE
import os as _os
_TEST_NT = int(_os.environ.get('K_NT', '0'))
MSTOP = int(_os.environ.get('K_MSTOP', '0'))
if _TEST_NT:
    NT = _TEST_NT
MT = 128
DI = 2048
DS = 16
NH = 16
HD = 64
FD = 128
N_SUBLAYERS = int(_os.environ.get('K_NSUB', '4'))

ENGINES = ("tensor", "vector", "scalar", "gpsimd", "sync")
NRING = {"sync": 24, "gpsimd": 8}


class Op:
    __slots__ = ("engine", "fn", "args", "kwargs", "reads", "writes", "kind",
                 "deps", "need_inc", "sem", "val", "inc_amt", "pre_wait",
                 "ptr_reads", "late_keys", "pos")

    def __init__(self, engine, fn, args, kwargs, reads, writes, kind):
        self.engine, self.fn, self.args, self.kwargs = engine, fn, args, kwargs
        self.reads, self.writes, self.kind = reads, writes, kind
        self.deps = ()
        self.need_inc = False
        self.sem = None
        self.val = 0
        self.inc_amt = 1
        self.pre_wait = None
        self.ptr_reads = frozenset()
        self.late_keys = frozenset()
        self.pos = 0


class Buf:

    def __init__(self, nc, name, shape, dtype=F32, gran=None, psum=False):
        if psum:
            self.h = nc.alloc_psum_tensor(name, shape, dtype)
        else:
            self.h = nc.alloc_sbuf_tensor(name, shape, dtype)
        self.name = name
        self.cols = shape[-1]
        self.parts = shape[0]
        self.gran = gran if gran else self.cols

    def keys(self, lo, size):
        g = self.gran
        return frozenset(f"{self.name}:{i}" for i in range(lo // g, (lo + size + g - 1) // g))

    def s(self, lo=0, size=None, p0=0, pn=None):
        if size is None:
            size = self.cols - lo
        if pn is None:
            pn = self.parts - p0
        return (self.h[p0:p0 + pn, lo:lo + size], self.keys(lo, size))

    def ap(self, lo=0, size=None, p0=0, pn=None):
        return self.s(lo, size, p0, pn)[0]


class Ctx:
    def __init__(self, nc):
        self.nc = nc
        self.ops = []
        self.eng_sem = {e: nc.alloc_semaphore(f"es_{e}") for e in ENGINES}
        self.dma_sems = {e: [nc.alloc_semaphore(f"ds_{e}_{i}") for i in range(NRING[e])]
                         for e in NRING}
        self.dma_idx = {e: 0 for e in NRING}
        self.eng_pos = {}
        self.store_ops = []

    def op(self, engine, fn, args, kwargs=None, reads=(), writes=(), kind="compute",
           ptr_reads=(), late=()):
        rd, wr, pr, lt = set(), set(), set(), set()
        for x in reads:
            rd |= x
        for x in writes:
            wr |= x
        for x in ptr_reads:
            rd |= x
            pr |= x
        for x in late:
            lt |= x
        o = Op(engine, fn, list(args), dict(kwargs or {}), frozenset(rd), frozenset(wr), kind)
        if kind == "compute" and args and hasattr(args[0], "shape"):
            shp = args[0].shape
            free = 1
            for d_ in shp[1:]:
                free *= d_
            if free <= 16:
                lt |= wr
        o.ptr_reads = frozenset(pr)
        o.late_keys = frozenset(lt)
        o.pos = self.eng_pos.get(engine, 0)
        self.eng_pos[engine] = o.pos + 1
        self.ops.append(o)
        return o

    def load(self, dst, src_ap, engine="sync"):
        dap, dk = dst
        return self.op(engine, "dma_start", [dap, src_ap], {}, [], [dk], kind="dma")

    def store(self, dst_ap, src, engine="gpsimd"):
        sap, sk = src
        o = self.op(engine, "dma_start", [dst_ap, sap], {}, [sk], [], kind="dma")
        self.store_ops.append(o)
        return o

    def mm(self, out, lhsT, rhs, start=True, stop=True):
        oap, ok = out
        lap, lk = lhsT
        rap, rk = rhs
        return self.op("tensor", "matmul", [oap, lap, rap],
                       dict(start=start, stop=stop), [lk, rk], [ok])

    def tr(self, out, in_, ident):
        oap, ok = out
        iap, ik = in_
        return self.op("tensor", "transpose", [oap, iap, ident[0]], {}, [ik, ident[1]], [ok])

    def cp(self, eng, out, in_):
        oap, ok = out
        iap, ik = in_
        fn = "tensor_copy" if eng in ("vector", "gpsimd") else "copy"
        return self.op(eng, fn, [oap, iap], {}, [ik], [ok])

    def tt(self, eng, out, in0, in1, op):
        oap, ok = out
        return self.op(eng, "tensor_tensor", [oap, in0[0], in1[0]], dict(op=op),
                       [in0[1], in1[1]], [ok])

    def ts(self, eng, out, in0, s1, s2, op0, op1=None):
        oap, ok = out
        reads = [in0[1]]
        s1a = s1
        if isinstance(s1, tuple):
            s1a = s1[0]; reads.append(s1[1])
        s2a = s2
        if isinstance(s2, tuple):
            s2a = s2[0]; reads.append(s2[1])
        if op1 is None:
            return self.op(eng, "tensor_scalar", [oap, in0[0], s1a, None, op0], {},
                           [in0[1]], [ok], ptr_reads=reads[1:])
        return self.op(eng, "tensor_scalar", [oap, in0[0], s1a, s2a, op0, op1], {},
                       [in0[1]], [ok], ptr_reads=reads[1:])

    def ts_accum(self, eng, out, in0, s1, op0, accum_out, seed=0.0):
        oap, ok = out
        aap, akk = accum_out
        reads = [in0[1]]
        s1a = s1
        if isinstance(s1, tuple):
            s1a = s1[0]; reads.append(s1[1])
        return self.op(eng, "tensor_scalar", [oap, in0[0], s1a, seed, op0, OP.add],
                       dict(accum_out=aap), [in0[1]], [ok, akk],
                       ptr_reads=reads[1:], late=[akk])

    def stt(self, eng, out, in0, scalar, in1, op0, op1):
        oap, ok = out
        reads = [in0[1], in1[1]]
        sa = scalar
        if isinstance(scalar, tuple):
            sa = scalar[0]; reads.append(scalar[1])
        return self.op(eng, "scalar_tensor_tensor", [oap, in0[0], sa, in1[0], op0, op1],
                       {}, [in0[1], in1[1]], [ok], ptr_reads=reads[2:])

    def act(self, out, in_, func, bias=None, scale=None, accum_out=None):
        oap, ok = out
        reads = [in_[1]]
        preads = []
        kw = {}
        if bias is not None:
            if isinstance(bias, tuple):
                kw["bias"] = bias[0]; preads.append(bias[1])
            else:
                kw["bias"] = bias
        if scale is not None:
            if isinstance(scale, tuple):
                kw["scale"] = scale[0]; preads.append(scale[1])
            else:
                kw["scale"] = scale
        writes = [ok]
        late = []
        if accum_out is not None:
            kw["accum_out"] = accum_out[0]
            writes.append(accum_out[1])
            late.append(accum_out[1])
        return self.op("scalar", "activation", [oap, in_[0], func], kw, reads, writes,
                       ptr_reads=preads, late=late)

    def scan(self, out, d0, d1, init, op0=OP.mult, op1=OP.add):
        oap, ok = out
        return self.op("vector", "tensor_tensor_scan", [oap, d0[0], d1[0], init[0], op0, op1],
                       {}, [d0[1], d1[1]], [ok], ptr_reads=[init[1]], late=[ok])

    def red(self, out, in_, op=OP.add, fn=None):
        oap, ok = out
        if fn:
            return self.op("vector", fn, [oap, in_[0]], dict(axis=AX.X), [in_[1]], [ok],
                           late=[ok])
        return self.op("vector", "tensor_reduce", [oap, in_[0]], dict(axis=AX.X, op=op),
                       [in_[1]], [ok], late=[ok])

    def memset(self, eng, dst, val):
        dap, dk = dst
        return self.op(eng, "memset", [dap, val], {}, [], [dk])

    def barrier(self):
        last = {}
        for i, o in enumerate(self.ops):
            if o.kind != "nop":
                last[o.engine] = i
        for e in ENGINES:
            o = self.op(e, "barrier_wait", [], {}, [], [], kind="nop")
            o.deps = tuple(last[e2] for e2 in ENGINES if e2 != e and e2 in last)
            for j in o.deps:
                self.ops[j].need_inc = True
        return last

    def resolve(self):
        last_w = {}
        readers = {}
        for i, o in enumerate(self.ops):
            if o.kind == "nop":
                last_w.clear()
                readers.clear()
                continue
            dep = set()
            W = 48
            for k in o.reads:
                j = last_w.get(k)
                if j is None:
                    continue
                w = self.ops[j]
                if w.engine != o.engine:
                    dep.add(j)
                elif (k in o.ptr_reads or k in w.late_keys) and (o.pos - w.pos) < W:
                    dep.add(j)
            for k in o.writes:
                j = last_w.get(k)
                if j is not None and self.ops[j].engine != o.engine:
                    dep.add(j)
                for e2, j in readers.get(k, {}).items():
                    if e2 != o.engine:
                        dep.add(j)
            o.deps = tuple(dep)
            for j in dep:
                self.ops[j].need_inc = True
            for k in o.reads:
                readers.setdefault(k, {})[o.engine] = i
            for k in o.writes:
                last_w[k] = i
                readers[k] = {}

        cnt = {e: 0 for e in ENGINES}
        for o in self.ops:
            if o.kind == "dma":
                e = o.engine
                j = self.dma_idx[e]
                nr = NRING[e]
                o.sem = self.dma_sems[e][j % nr]
                o.val = 16 * (j // nr + 1)
                o.inc_amt = 16
                o.need_inc = True
                if j >= nr:
                    o.pre_wait = (self.dma_sems[e][j % nr], 16 * (j // nr))
                self.dma_idx[e] += 1
            elif o.need_inc and o.kind != "nop":
                cnt[o.engine] += 1
                o.sem = self.eng_sem[o.engine]
                o.val = cnt[o.engine]
                o.inc_amt = 1

    def emit(self):
        self.resolve()
        streams = {e: [] for e in ENGINES}
        for o in self.ops:
            streams[o.engine].append(o)
        ops_by_idx = self.ops
        nc = self.nc
        with nc.Block() as block:
            def emit_engine(eng_obj, eng_name):
                waited = {}
                for o in streams[eng_name]:
                    tgt = {}
                    for j in o.deps:
                        d = ops_by_idx[j]
                        assert d.sem is not None, (o.fn, d.fn)
                        key = id(d.sem)
                        if tgt.get(key, (None, -1))[1] < d.val:
                            tgt[key] = (d.sem, d.val)
                    if o.pre_wait is not None:
                        sem, val = o.pre_wait
                        key = id(sem)
                        if tgt.get(key, (None, -1))[1] < val:
                            tgt[key] = (sem, val)
                    for sem, val in tgt.values():
                        if waited.get(id(sem), -1) < val:
                            eng_obj.wait_ge(sem, val)
                            waited[id(sem)] = val
                    if o.kind == "nop":
                        continue
                    ins = getattr(eng_obj, o.fn)(*o.args, **o.kwargs)
                    if o.need_inc:
                        ins.then_inc(o.sem, o.inc_amt)
                if eng_name == "gpsimd":
                    fin = {}
                    for o in self.store_ops:
                        if fin.get(id(o.sem), (None, -1))[1] < o.val:
                            fin[id(o.sem)] = (o.sem, o.val)
                    for sem, val in fin.values():
                        if waited.get(id(sem), -1) < val:
                            eng_obj.wait_ge(sem, val)

            @block.tensor
            def _(t):
                emit_engine(t, "tensor")

            @block.vector
            def _(v):
                emit_engine(v, "vector")

            @block.scalar
            def _(s):
                emit_engine(s, "scalar")

            @block.gpsimd
            def _(g):
                emit_engine(g, "gpsimd")

            @block.sync
            def _(s):
                emit_engine(s, "sync")



def build_program():
    nc = bass.Bass("TRN2", target_bir_lowering=False, detect_race_conditions=False)

    d = {}
    def din(name, shape):
        d[name] = nc.dram_tensor(name, list(shape), F32, kind="ExternalInput")
        return d[name]

    x_d = din("x", (S, DM))
    wq_d = din("wq", (DM, DM)); wk_d = din("wk", (DM, DM))
    wv_d = din("wv", (DM, DM)); wo_d = din("wo", (DM, DM))
    rf2_d = din("rf2", (128, 128))
    gw_d = din("gw", (DM, 8))
    w0T_d = din("w0T", (DM, DM)); w1T_d = din("w1T", (DM, DM))
    b0_d = din("b0r", (1, DM)); b1_d = din("b1r", (1, DM))
    normw_d = din("normw", (128, 8))
    inpT_d = din("inpT", (DM, 2 * DI))
    convw_d = din("convw", (128, 64)); convb_d = din("convb", (128, 16))
    xpT_d = din("xpT", (DI, 96))
    dtpT_d = din("dtpT", (64, DI)); dtb_d = din("dtb", (128, 16))
    alog_d = din("alog", (128, 256)); dvec_d = din("dvec", (128, 16))
    opT_d = din("opT", (DI, DM))
    ident_d = din("ident", (128, 128))
    ones16_d = din("ones16", (16, 128))
    shp_d = din("shp", (128, 1)); b1v_d = din("b1v", (128, 1)); b2v_d = din("b2v", (128, 1))
    epsv_d = din("epsv", (128, 2))
    out_d = nc.dram_tensor("out", [S, DM], F32, kind="ExternalOutput")
    DBG = int(_os.environ.get('K_DBGOUT', '0'))
    dbg_d = {}
    if DBG:
        for nm in ("dKVZ", "dC", "dE", "dF", "dD", "dkp", "dvtm", "dkptm", "dKVT", "dKsum",
                   "pE", "pF", "pC", "pkp", "pkptm", "pvtm", "qC", "qln"):
            dbg_d[nm] = nc.dram_tensor(nm, [128, 4096], F32, kind="ExternalOutput")

    A = Buf(nc, "slotA", [128, 4096], gran=256)
    B = Buf(nc, "slotB", [128, 4096], gran=256)
    C = Buf(nc, "slotC", [128, 4096], gran=256)
    Dd = Buf(nc, "slotD", [128, 4096], gran=256)
    E = Buf(nc, "slotE", [128, 4096], gran=256)
    Fb = Buf(nc, "slotF", [128, 4096], gran=256)
    G = Buf(nc, "slotG", [128, 4096], gran=256)
    H = Buf(nc, "slotH", [128, 4096], gran=256)

    NR_W = 5
    WR = Buf(nc, "wring", [128, NR_W * 1024], gran=1024)

    ident = Buf(nc, "ident_sb", [128, 128])
    rf2 = Buf(nc, "rf2_sb", [128, 128])
    ones16 = Buf(nc, "ones16_sb", [16, 128])
    onescol = Buf(nc, "onescol", [128, 1])
    gwS = Buf(nc, "gw_sb", [128, 64], gran=8)
    b0r = Buf(nc, "b0r_sb", [1, DM])
    b1r = Buf(nc, "b1r_sb", [1, DM])
    normw = Buf(nc, "normw_sb", [128, 8], gran=1)
    convw = Buf(nc, "convw_sb", [128, 64], gran=4)
    convb = Buf(nc, "convb_sb", [128, 16], gran=1)
    dtb = Buf(nc, "dtb_sb", [128, 16], gran=1)
    dvec = Buf(nc, "dvec_sb", [128, 16], gran=1)
    Asb = Buf(nc, "A_sb", [128, 256], gran=16)
    dtpT = Buf(nc, "dtpT_sb", [64, DI], gran=128)
    xpT = Buf(nc, "xpT_sb", [128, 16 * 96], gran=96)
    shp = Buf(nc, "shp_sb", [128, 1])
    b1v = Buf(nc, "b1v_sb", [128, 1])
    b2v = Buf(nc, "b2v_sb", [128, 1])
    KVZ = Buf(nc, "KVZ_sb", [128, NH * 65], gran=65)
    lnv = Buf(nc, "lnv", [128, 24], gran=1)
    kp = Buf(nc, "kp", [128, 512])
    sg1 = Buf(nc, "sg1", [128, 1024], gran=512)
    sg2 = Buf(nc, "sg2", [128, 1024], gran=512)
    kptm = Buf(nc, "kptm", [128, 512], gran=128)
    vtm = Buf(nc, "vtm", [128, 520], gran=65)
    esc = Buf(nc, "esc", [128, 512], gran=128)
    gl = Buf(nc, "gl", [128, 8])
    gl0 = Buf(nc, "gl0", [128, 8])
    gmk = Buf(nc, "gmk", [128, 8])
    ge = Buf(nc, "ge", [128, 8])
    gp = Buf(nc, "gp", [128, 2])
    gv = Buf(nc, "gv", [128, 16], gran=1)
    xdbl = Buf(nc, "xdbl", [128, 192], gran=96)
    dlrfm = Buf(nc, "dlrfm", [64, 256])
    BCfm = Buf(nc, "BCfm", [16, 512], gran=256)
    convs = Buf(nc, "convs", [128, 392], gran=8)
    tmp6 = Buf(nc, "tmp6", [128, 6])
    tails = Buf(nc, "tails", [128, 48], gran=3)
    dAb = Buf(nc, "dAb", [128, 1024], gran=128)
    yv = Buf(nc, "yv", [128, 256])
    state = Buf(nc, "state", [128, 256], gran=16)
    zsc = Buf(nc, "zsc", [128, 128], gran=1)
    brow = Buf(nc, "brow", [1, 512], gran=256)
    epsv = Buf(nc, "epsv_sb", [128, 2], gran=1)

    PS = [Buf(nc, f"ps{i}", [128, 512], psum=True) for i in range(8)]

    ctx = Ctx(nc)
    psum_i = [0]

    def nb():
        b = PS[psum_i[0] % 8]
        psum_i[0] += 1
        return b

    wslot = [0]
    def fetch(dram, r0, c0, ncols=1024):
        sl = wslot[0] % NR_W
        wslot[0] += 1
        dst = WR.s(sl * 1024, ncols)
        ctx.load(dst, dram[r0:r0 + 128, c0:c0 + ncols])
        return dst

    ctx.load(ident.s(), ident_d[:, :])
    ctx.load(rf2.s(), rf2_d[:, :])
    ctx.load(ones16.s(), ones16_d[:, :])
    for ki in range(8):
        ctx.load(gwS.s(ki * 8, 8), gw_d[ki * 128:(ki + 1) * 128, :])
    ctx.load(b0r.s(), b0_d[:, :])
    ctx.load(b1r.s(), b1_d[:, :])
    ctx.load(normw.s(), normw_d[:, :])
    ctx.load(convw.s(), convw_d[:, :])
    ctx.load(convb.s(), convb_d[:, :])
    ctx.load(dtb.s(), dtb_d[:, :])
    ctx.load(dvec.s(), dvec_d[:, :])
    ctx.load(Asb.s(), alog_d[:, :])
    ctx.load(dtpT.s(), dtpT_d[:, :])
    for kj in range(16):
        ctx.load(xpT.s(kj * 96, 96), xpT_d[kj * 128:(kj + 1) * 128, :])
    ctx.load(shp.s(), shp_d[:, :])
    ctx.load(b1v.s(), b1v_d[:, :])
    ctx.load(b2v.s(), b2v_d[:, :])
    ctx.load(epsv.s(), epsv_d[:, :])
    ctx.memset("gpsimd", onescol.s(), 1.0)
    ctx.op("gpsimd", "memset", [bass.AP(vtm.h, 64, [[vtm.cols, 128], [130, 4], [65, 2]]), 1.0],
           {}, [], [vtm.keys(0, 520)])
    ctx.memset("gpsimd", G.s(0, 2048), 0.0)
    ctx.memset("gpsimd", state.s(), 0.0)
    ctx.memset("gpsimd", lnv.s(), 0.0)
    ctx.memset("gpsimd", zsc.s(), 0.0)
    ctx.memset("gpsimd", gv.s(), 0.0)
    ctx.memset("gpsimd", tails.s(), 0.0)
    ctx.act(Asb.s(), Asb.s(), AF.Exp)
    ctx.ts("vector", Asb.s(), Asb.s(), -1.0, None, OP.mult)


    def load_x_tile(T):
        for t_s in range(4):
            ctx.load(A.s(t_s * 1024, 1024),
                     x_d[T * 512 + t_s * 128: T * 512 + (t_s + 1) * 128, :])

    def layer_norm(src, t_s, ln0):
        c_s, c_m, c_q, c_r = ln0, ln0 + 1, ln0 + 2, ln0 + 3
        ctx.red(lnv.s(c_s, 1), src.s(t_s * 1024, 1024), fn="reduce_sum")
        ctx.ts("vector", lnv.s(c_m, 1), lnv.s(c_s, 1), -1.0 / DM, None, OP.mult)
        ctx.ts("vector", C.s(t_s * 1024, 1024), src.s(t_s * 1024, 1024),
               lnv.s(c_m, 1), None, OP.add)
        ctx.act(Dd.s(0, 1024), C.s(t_s * 1024, 1024), AF.Square, accum_out=lnv.s(c_q, 1))
        ctx.act(lnv.s(c_r, 1), lnv.s(c_q, 1), AF.Ln, bias=epsv.s(0, 1), scale=1.0 / DM)
        ctx.act(lnv.s(c_r, 1), lnv.s(c_r, 1), AF.Exp, scale=-0.5)
        ctx.ts("vector", C.s(t_s * 1024, 1024), C.s(t_s * 1024, 1024),
               lnv.s(c_r, 1), None, OP.mult)

    def transpose_tile(src, dst):
        for dc in range(8):
            bank = nb()
            for t_s in range(4):
                ctx.tr((bank.ap(t_s * 128, 128), bank.keys(0, 512)),
                       src.s(t_s * 1024 + dc * 128, 128), ident.s())
            ctx.cp("vector", dst.s(dc * 512, 512), (bank.ap(0, 512), bank.keys(0, 512)))

    def proj1024(w_dram, src_fm, dst_fm, eng_evac="scalar"):
        banks = [nb() for _ in range(8)]
        for ki in range(8):
            wch = fetch(w_dram, ki * 128, 0, 1024)
            for mo in range(8):
                lhsT = (wch[0][:, mo * 128:(mo + 1) * 128], wch[1])
                ctx.mm((banks[mo].ap(), banks[mo].keys(0, 512)), lhsT,
                       src_fm.s(ki * 512, 512), start=(ki == 0), stop=(ki == 7))
        for mo in range(8):
            ctx.cp(eng_evac, dst_fm.s(mo * 512, 512),
                   (banks[mo].ap(), banks[mo].keys(0, 512)))

    def phi_front(src_fm, h):
        mo, poff, par = h // 2, (h % 2) * 64, h % 2
        bank = nb()
        pref = (bank.ap(0, 512), bank.keys(0, 512))
        ctx.mm(pref, rf2.s(0, 128, p0=poff, pn=64),
               src_fm.s(mo * 512, 512, p0=poff, pn=64))
        ctx.act(sg1.s(par * 512, 512), pref, AF.Sign, bias=b1v.s())
        ctx.act(sg2.s(par * 512, 512), pref, AF.Sign, bias=b2v.s(), scale=-1.0)
        return pref

    def phi_back(pref, h, dst):
        par = h % 2
        g1, g2 = sg1.s(par * 512, 512), sg2.s(par * 512, 512)
        ctx.tt("vector", g2, g2, g1, OP.subtract)
        ctx.ts("vector", g2, g2, float(np.pi), None, OP.mult)
        ctx.tt("vector", dst.s(), g2, pref, OP.add)
        ctx.ts("vector", dst.s(0, 512, p0=0, pn=64), dst.s(0, 512, p0=0, pn=64),
               float(np.pi) / 2, None, OP.add)
        CL = float(np.float32(3.1415925))
        ctx.ts("vector", dst.s(), dst.s(), -CL, CL, OP.max, OP.min)
        ctx.act(dst.s(), dst.s(), AF.Sin)

    def ffn_block():
        transpose_tile(C, Dd)
        for t_s in range(4):
            gbank = nb()
            gref = (gbank.ap(0, 8), gbank.keys(0, 512))
            for ki in range(8):
                ctx.mm(gref, Dd.s(ki * 512 + t_s * 128, 128), gwS.s(ki * 8, 8),
                       start=(ki == 0), stop=(ki == 7))
            ctx.cp("vector", gl0.s(), gref)
            ctx.red(gv.s(0, 1), gref, fn="reduce_max")
            ctx.ts("vector", gl.s(), gref, gv.s(0, 1), None, OP.subtract)
            ctx.act(ge.s(), gl.s(), AF.Exp)
            ctx.red(gv.s(1, 1), ge.s(), fn="reduce_sum")
            ctx.op("vector", "reciprocal", [gv.ap(2, 1), gv.ap(1, 1)], {},
                   [gv.keys(1, 1)], [gv.keys(2, 1)], late=[gv.keys(2, 1)])
            ctx.ts("vector", gp.s(), ge.s(0, 2), gv.s(2, 1), None, OP.mult)
            ctx.tt("vector", gv.s(3, 1), gl0.s(0, 1), gv.s(0, 1), OP.is_ge)
            ctx.ts_accum("vector", gmk.s(), gl0.s(), gl0.s(1, 1),
                         OP.is_gt, gv.s(4, 1))
            ctx.tt("vector", gv.s(5, 1), gl0.s(0, 1), gl0.s(1, 1), OP.is_equal)
            ctx.tt("vector", gv.s(6, 1), gv.s(4, 1), gv.s(5, 1), OP.add)
            ctx.op("vector", "tensor_single_scalar",
                   [gv.ap(7, 1), gv.ap(6, 1), 1.0, OP.is_equal], {},
                   [gv.keys(6, 1)], [gv.keys(7, 1)])
            ctx.tt("vector", gv.s(8 + t_s, 1), gp.s(0, 1), gv.s(3, 1), OP.mult)
            ctx.tt("vector", gv.s(12 + t_s, 1), gp.s(1, 1), gv.s(7, 1), OP.mult)
        for ex, (w_dram, brow, ccol0) in enumerate([(w0T_d, b0r, 8), (w1T_d, b1r, 12)]):
            for jb in range(2):
                ebanks = [nb() for _ in range(4)]
                for ki in range(8):
                    wch = fetch(w_dram, ki * 128, jb * 512, 512)
                    for t_s in range(4):
                        ctx.mm((ebanks[t_s].ap(), ebanks[t_s].keys(0, 512)),
                               Dd.s(ki * 512 + t_s * 128, 128), wch,
                               start=(ki == 0), stop=False)
                for t_s in range(4):
                    ctx.mm((ebanks[t_s].ap(), ebanks[t_s].keys(0, 512)),
                           ones16.s(0, 128, p0=0, pn=1),
                           brow.s(jb * 512, 512), start=False, stop=True)
                    eslc = C.s(t_s * 1024 + jb * 512, 512)
                    ctx.act(eslc, (ebanks[t_s].ap(), ebanks[t_s].keys(0, 512)),
                            AF.Silu)
                    ctx.stt("vector", B.s(t_s * 1024 + jb * 512, 512), eslc,
                            gv.s(ccol0 + t_s, 1), B.s(t_s * 1024 + jb * 512, 512),
                            OP.mult, OP.add)

    for T in range(NT):
        load_x_tile(T)
        for t_s in range(4):
            layer_norm(A, t_s, 4 * t_s)
        if DBG and T == 0:
            ctx.store(dbg_d["qC"][:, :], C.s())
            ctx.store(dbg_d["qln"][:, 0:16], lnv.s())
        transpose_tile(C, Dd)
        proj1024(wk_d, Dd, E)
        proj1024(wv_d, Dd, Fb)
        pref_next = phi_front(E, 0)
        for mo in range(8):
            vb = nb()
            for t_s in range(4):
                ctx.tr((vb.ap(t_s * 128, 128), vb.keys(0, 512)),
                       Fb.s(mo * 512 + t_s * 128, 128), ident.s())
            ctx.op("vector", "tensor_copy",
                   [bass.AP(vtm.h, 0, [[vtm.cols, 128], [130, 4], [65, 2], [1, 64]]),
                    bass.AP(vb.h, 0, [[vb.cols, 128], [128, 4], [64, 2], [1, 64]])],
                   {}, [vb.keys(0, 512)], [vtm.keys(0, 520)])
            for hh in range(2):
                h = mo * 2 + hh
                pref = pref_next
                if h + 1 < NH:
                    pref_next = phi_front(E, h + 1)
                phi_back(pref, h, kp)
                kb = nb()
                for t_s in range(4):
                    ctx.tr((kb.ap(t_s * 128, 128), kb.keys(0, 512)),
                           kp.s(t_s * 128, 128), ident.s())
                ctx.cp("vector", kptm.s(0, 512), (kb.ap(0, 512), kb.keys(0, 512)))
                kvb = nb()
                kvref = (kvb.ap(0, 65), kvb.keys(0, 512))
                for t_s in range(4):
                    ctx.mm(kvref, kptm.s(t_s * 128, 128),
                           vtm.s(t_s * 130 + hh * 65, 65),
                           start=(t_s == 0), stop=(t_s == 3))
                ctx.tt("vector", G.s(h * 65, 65), kvref, G.s(h * 65, 65), OP.add)

    ctx.barrier()
    if DBG:
        ctx.store(dbg_d["pE"][:, :], E.s())
        ctx.store(dbg_d["pF"][:, :], Fb.s())
        ctx.store(dbg_d["pC"][:, :], C.s())
        ctx.store(dbg_d["pkp"][:, 0:512], kp.s())
        ctx.store(dbg_d["pkptm"][:, 0:512], kptm.s())
        ctx.store(dbg_d["pvtm"][:, 0:512], vtm.s())
        ctx.barrier()
    ctx.cp("vector", KVZ.s(0, NH * 65), G.s(0, NH * 65))
    ctx.barrier()

    for T in range(NT):
        load_x_tile(T)
        for t_s in range(4):
            layer_norm(A, t_s, 4 * t_s)
        transpose_tile(C, Dd)
        proj1024(wq_d, Dd, E)
        pref_next = phi_front(E, 0)
        for h in range(NH):
            pref = pref_next
            if h + 1 < NH:
                pref_next = phi_front(E, h + 1)
            phi_back(pref, h, kp)
            hq = (h % 4) * 16
            qbs = [nb() for _ in range(4)]
            for t_s in range(4):
                qref = (qbs[t_s].ap(0, 65), qbs[t_s].keys(0, 512))
                ctx.mm(qref, kp.s(t_s * 128, 128), KVZ.s(h * 65, 65))
            for t_s in range(4):
                ctx.ts("vector", zsc.s(hq + t_s, 1),
                       (qbs[t_s].ap(64, 1), qbs[t_s].keys(0, 512)),
                       float(128e-6), None, OP.max)
            ctx.op("vector", "reciprocal", [zsc.ap(hq + 4, 4), zsc.ap(hq, 4)], {},
                   [zsc.keys(hq, 4)], [zsc.keys(hq + 4, 4)], late=[zsc.keys(hq + 4, 4)])
            for t_s in range(4):
                ctx.ts("vector", Fb.s(t_s * 1024 + h * 64, 64),
                       (qbs[t_s].ap(0, 64), qbs[t_s].keys(0, 512)),
                       zsc.s(hq + 4 + t_s, 1), None, OP.mult)
        transpose_tile(Fb, Dd)
        banks = [nb() for _ in range(8)]
        for ki in range(8):
            wch = fetch(wo_d, ki * 128, 0, 1024)
            for i8, (t_s, ob) in enumerate([(a, b) for a in range(4) for b in range(2)]):
                lhsT = Dd.s(ki * 512 + t_s * 128, 128)
                rhs = (wch[0][:, ob * 512:(ob + 1) * 512], wch[1])
                ctx.mm((banks[i8].ap(), banks[i8].keys(0, 512)), lhsT, rhs,
                       start=(ki == 0), stop=(ki == 7))
        for i8, (t_s, ob) in enumerate([(a, b) for a in range(4) for b in range(2)]):
            ctx.tt("vector", B.s(t_s * 1024 + ob * 512, 512),
                   (banks[i8].ap(), banks[i8].keys(0, 512)),
                   A.s(t_s * 1024 + ob * 512, 512), OP.add)

        if N_SUBLAYERS >= 2:
            for t_s in range(4):
                layer_norm(B, t_s, 4 * t_s)
            ffn_block()

        if N_SUBLAYERS >= 3:
            for t_s in range(4):
                layer_norm(B, t_s, 4 * t_s)
                ctx.tt("vector", B.s(t_s * 1024, 1024), B.s(t_s * 1024, 1024),
                       C.s(t_s * 1024, 1024), OP.add)
                ctx.act(Dd.s(0, 1024), C.s(t_s * 1024, 1024), AF.Square,
                        accum_out=lnv.s(16, 1))
                ctx.act(lnv.s(17, 1), lnv.s(16, 1), AF.Ln, bias=epsv.s(0, 1), scale=1.0 / DM)
                ctx.act(lnv.s(17, 1), lnv.s(17, 1), AF.Exp, scale=-0.5)
                ctx.ts("vector", C.s(t_s * 1024, 1024), C.s(t_s * 1024, 1024),
                       lnv.s(17, 1), None, OP.mult)
            transpose_tile(C, Dd)
            for dc8 in range(8):
                ctx.ts("vector", Dd.s(dc8 * 512, 512), Dd.s(dc8 * 512, 512),
                       normw.s(dc8, 1), None, OP.mult)
            def m_front(mm_i):
                    pp = mm_i % 2
                    hb = pp * 2048
                    for jg in range(4):
                        banks = [nb() for _ in range(8)]
                        for ki in range(8):
                            wch = fetch(inpT_d, ki * 128, jg * 1024, 1024)
                            for jcl in range(8):
                                lhsT = (wch[0][:, jcl * 128:(jcl + 1) * 128], wch[1])
                                ctx.mm((banks[jcl].ap(0, 128), banks[jcl].keys(0, 512)),
                                       lhsT, Dd.s(ki * 512 + mm_i * 128, 128),
                                       start=(ki == 0), stop=(ki == 7))
                        for jcl in range(8):
                            jc = jg * 8 + jcl
                            dstbuf, dcc = (E, jc) if jc < 16 else (Fb, jc - 16)
                            ctx.cp("scalar", dstbuf.s(hb + dcc * 128, 128),
                                   (banks[jcl].ap(0, 128), banks[jcl].keys(0, 512)))
                    for dc in range(16):
                        eb0 = hb + dc * 128
                        ctx.cp("vector", convs.s(0, 3), tails.s(dc * 3, 3))
                        ctx.cp("vector", convs.s(3, 128), E.s(eb0, 128))
                        ctx.cp("vector", tails.s(dc * 3, 3), E.s(eb0 + 125, 3))
                        ctx.ts("vector", E.s(eb0, 128), convs.s(0, 128),
                               convw.s(dc * 4 + 0, 1), convb.s(dc, 1), OP.mult, OP.add)
                        for k in range(1, 4):
                            ctx.stt("vector", E.s(eb0, 128), convs.s(k, 128),
                                    convw.s(dc * 4 + k, 1), E.s(eb0, 128), OP.mult, OP.add)
                    ctx.act(E.s(hb, 2048), E.s(hb, 2048), AF.Silu)
                    if mm_i >= 1:
                        po = (1 - pp) * 2048
                        ctx.act(Fb.s(po, 2048), Fb.s(po, 2048), AF.Silu)
                    xb = nb()
                    xref = (xb.ap(0, 96), xb.keys(0, 512))
                    for kj in range(16):
                        ctx.mm(xref, E.s(hb + kj * 128, 128), xpT.s(kj * 96, 96),
                               start=(kj == 0), stop=(kj == 15))
                    ctx.cp("vector", xdbl.s(pp * 96, 96), xref)
                    tb = nb()
                    ctx.tr((tb.ap(0, 128, p0=0, pn=64), tb.keys(0, 512)),
                           xdbl.s(pp * 96, 64), ident.s())
                    ctx.cp("vector", dlrfm.s(pp * 128, 128, p0=0, pn=64),
                           (tb.ap(0, 128, p0=0, pn=64), tb.keys(0, 512)))
                    tb2 = nb()
                    ctx.tr((tb2.ap(0, 128, p0=0, pn=16), tb2.keys(0, 512)),
                           xdbl.s(pp * 96 + 64, 16), ident.s())
                    ctx.cp("vector", BCfm.s(pp * 256, 128, p0=0, pn=16),
                           (tb2.ap(0, 128, p0=0, pn=16), tb2.keys(0, 512)))
                    tb3 = nb()
                    ctx.tr((tb3.ap(0, 128, p0=0, pn=16), tb3.keys(0, 512)),
                           xdbl.s(pp * 96 + 80, 16), ident.s())
                    ctx.cp("vector", BCfm.s(pp * 256 + 128, 128, p0=0, pn=16),
                           (tb3.ap(0, 128, p0=0, pn=16), tb3.keys(0, 512)))
                    for dc in range(16):
                        db = nb()
                        dref = (db.ap(0, 128), db.keys(0, 512))
                        ctx.mm(dref, dtpT.s(dc * 128, 128), dlrfm.s(pp * 128, 128))
                        ctx.act(esc.s(pp * 128, 128), dref, AF.Exp, bias=dtb.s(dc, 1))
                        ctx.act(G.s(hb + dc * 128, 128), esc.s(pp * 128, 128), AF.Ln,
                                bias=epsv.s(1, 1))
                    for n in range(DS):
                        xb1 = nb()
                        ctx.mm((xb1.ap(0, 256, p0=0, pn=1), xb1.keys(0, 512)),
                               ident.s(n, 1, p0=0, pn=16), BCfm.s(pp * 256, 256))
                        ctx.cp("vector", brow.s(pp * 256, 256),
                               (xb1.ap(0, 256, p0=0, pn=1), xb1.keys(0, 512)))
                        bb = nb()
                        ctx.mm((bb.ap(0, 256), bb.keys(0, 512)),
                               ones16.s(0, 128, p0=0, pn=1), brow.s(pp * 256, 256))
                        ctx.cp("scalar", C.s(hb + n * 128, 128), (bb.ap(0, 128), bb.keys(0, 512)))
                        ctx.op("scalar", "copy", [A.h[:, hb + n: hb + 2048: 16], bb.ap(128, 128)],
                               {}, [bb.keys(0, 512)], [A.keys(hb, 2048)])

            def m_back(mm_i):
                pp = mm_i % 2
                hb = pp * 2048
                if mm_i == 3:
                    ctx.act(Fb.s(hb, 2048), Fb.s(hb, 2048), AF.Silu)
                ho = 2048 - hb

                def du_part(dc):
                    ctx.tt("gpsimd", convs.s(136 + (dc % 2) * 128, 128),
                           G.s(hb + dc * 128, 128), E.s(hb + dc * 128, 128), OP.mult)

                def dbu_half(dc, half):
                    slot = (dc * 2 + half) % 2
                    du_rep = bass.AP(convs.h, 136 + (dc % 2) * 128,
                                     [[convs.cols, 128], [0, 8], [1, 128]])
                    ctx.op("gpsimd", "tensor_tensor",
                           [H.h[:, ho + slot * 1024: ho + slot * 1024 + 1024]
                            .rearrange("p (a b) -> p a b", a=8),
                            du_rep,
                            C.h[:, hb + half * 1024: hb + half * 1024 + 1024]
                            .rearrange("p (a b) -> p a b", a=8),
                            OP.mult],
                           {}, [convs.keys(136 + (dc % 2) * 128, 128),
                                C.keys(hb + half * 1024, 1024)],
                           [H.keys(ho + slot * 1024, 1024)])

                du_part(0)
                dbu_half(0, 0)
                for dc in range(16):
                    eb0 = hb + dc * 128
                    for half in range(2):
                        slot = (dc * 2 + half) % 2
                        for j in range(8):
                            n = half * 8 + j
                            ctx.act(dAb.s(j * 128, 128), G.s(eb0, 128), AF.Exp,
                                    scale=Asb.s(dc * 16 + n, 1))
                        for j in range(8):
                            n = half * 8 + j
                            ctx.op("vector", "tensor_tensor_scan",
                                   [H.h[:, hb + n: hb + 2048: 16], dAb.ap(j * 128, 128),
                                    H.h[:, ho + slot * 1024 + j * 128:
                                         ho + slot * 1024 + (j + 1) * 128],
                                    state.ap(dc * 16 + n, 1), OP.mult, OP.add], {},
                                   [dAb.keys(j * 128, 128),
                                    H.keys(ho + slot * 1024 + j * 128, 128)],
                                   [H.keys(hb, 2048)],
                                   ptr_reads=[state.keys(dc * 16 + n, 1)],
                                   late=[H.keys(hb, 2048)])
                        if half == 0:
                            dbu_half(dc, 1)
                        elif dc + 1 < 16:
                            du_part(dc + 1)
                            dbu_half(dc + 1, 0)
                    ctx.cp("vector", state.s(dc * 16, 16), H.s(hb + 2032, 16))
                    ctx.tt("vector", H.s(hb, 2048), H.s(hb, 2048), A.s(hb, 2048), OP.mult)
                    h3 = H.h[:, hb: hb + 2048].rearrange("p (t n) -> p t n", n=16)
                    ctx.op("vector", "tensor_reduce", [yv.ap(0, 128), h3],
                           dict(axis=AX.X, op=OP.add), [H.keys(hb, 2048)],
                           [yv.keys(0, 128)], late=[yv.keys(0, 128)])
                    ctx.stt("vector", yv.s(0, 128), E.s(eb0, 128),
                            dvec.s(dc, 1), yv.s(0, 128), OP.mult, OP.add)
                    ctx.tt("vector", E.s(eb0, 128), yv.s(0, 128),
                           Fb.s(eb0, 128), OP.mult)
                banks2 = [nb() for _ in range(2)]
                for kj in range(16):
                    wch = fetch(opT_d, kj * 128, 0, 1024)
                    for ob in range(2):
                        rhs = (wch[0][:, ob * 512:(ob + 1) * 512], wch[1])
                        ctx.mm((banks2[ob].ap(), banks2[ob].keys(0, 512)),
                               E.s(hb + kj * 128, 128), rhs,
                               start=(kj == 0), stop=(kj == 15))
                for ob in range(2):
                    col = mm_i * 1024 + ob * 512
                    ctx.tt("vector", B.s(col, 512),
                           (banks2[ob].ap(), banks2[ob].keys(0, 512)),
                           B.s(col, 512), OP.add)


            for k in range(5):
                if k < 4:
                    m_front(k)
                if k >= 1:
                    m_back(k - 1)
        if N_SUBLAYERS >= 4:
            for t_s in range(4):
                layer_norm(B, t_s, 4 * t_s)
            ffn_block()

        for t_s in range(4):
            ctx.store(out_d[T * 512 + t_s * 128: T * 512 + (t_s + 1) * 128, :],
                      B.s(t_s * 1024, 1024))
    if DBG:
        ctx.barrier()
        ctx.store(dbg_d["dKVZ"][:, 0:KVZ.cols], KVZ.s())
        ctx.store(dbg_d["dC"][:, :], C.s())
        ctx.store(dbg_d["dE"][:, :], E.s())
        ctx.store(dbg_d["dF"][:, :], Fb.s())
        ctx.store(dbg_d["dD"][:, :], Dd.s())
        ctx.store(dbg_d["dkp"][:, 0:512], kp.s())
        ctx.store(dbg_d["dvtm"][:, 0:512], vtm.s())
        ctx.store(dbg_d["dkptm"][:, 0:512], kptm.s())
        ctx.store(dbg_d["dKVT"][:, 0:2048], G.s(0, 2048))
        ctx.store(dbg_d["dKsum"][:, 0:16], Ksum.s())

    ctx.emit()
    return nc



_CACHE = {}

def host_shared(L0_attn_a_Wq, L0_attn_a_Wk, L0_attn_a_Wv, L0_attn_a_Wo,
                L0_attn_a_rf, L0_attn_b_norm_w, L0_attn_b_in_proj_w, L0_attn_b_conv_w,
                L0_attn_b_conv_b, L0_attn_b_x_proj_w, L0_attn_b_dt_proj_w,
                L0_attn_b_dt_proj_b, L0_attn_b_A_log, L0_attn_b_D, L0_attn_b_out_proj_w,
                L0_ffn_gate_w, L0_ffn_w0, L0_ffn_b0, L0_ffn_w1, L0_ffn_b1, **_ignore):
    f = np.float32
    cont = np.ascontiguousarray

    shp_v = np.zeros((128, 1), f); shp_v[0:64] = np.pi / 2
    pi = np.pi
    shared = {
        "wq": cont(L0_attn_a_Wq, dtype=f), "wk": cont(L0_attn_a_Wk, dtype=f),
        "wv": cont(L0_attn_a_Wv, dtype=f), "wo": cont(L0_attn_a_Wo, dtype=f),
        "rf2": cont(np.concatenate([L0_attn_a_rf, L0_attn_a_rf], axis=0), dtype=f),
        "gw": cont(L0_ffn_gate_w, dtype=f),
        "w0T": cont(L0_ffn_w0.T, dtype=f), "w1T": cont(L0_ffn_w1.T, dtype=f),
        "b0r": cont(L0_ffn_b0.reshape(1, DM), dtype=f),
        "b1r": cont(L0_ffn_b1.reshape(1, DM), dtype=f),
        "normw": cont(L0_attn_b_norm_w.reshape(8, 128).T, dtype=f),
        "inpT": cont(L0_attn_b_in_proj_w.T, dtype=f),
        "convw": cont(L0_attn_b_conv_w[:, 0, :].reshape(16, 128, 4)
                      .transpose(1, 0, 2).reshape(128, 64), dtype=f),
        "convb": cont(L0_attn_b_conv_b.reshape(16, 128).T, dtype=f),
        "xpT": cont(L0_attn_b_x_proj_w.T, dtype=f),
        "dtpT": cont(L0_attn_b_dt_proj_w.T, dtype=f),
        "dtb": cont(L0_attn_b_dt_proj_b.reshape(16, 128).T, dtype=f),
        "alog": cont(np.asarray(L0_attn_b_A_log).reshape(16, 128, 16)
                     .transpose(1, 0, 2).reshape(128, 256), dtype=f),
        "dvec": cont(L0_attn_b_D.reshape(16, 128).T, dtype=f),
        "opT": cont(L0_attn_b_out_proj_w.T, dtype=f),
        "ident": np.eye(128, dtype=f),
        "ones16": np.ones((16, 128), f),
        "shp": shp_v,
        "b1v": cont(shp_v - pi, dtype=f),
        "b2v": cont(-shp_v - pi, dtype=f),
        "epsv": cont(np.broadcast_to(np.array([1e-5, 1.0], f), (128, 2)), dtype=f),
    }
    return shared


def kernel(L0_attn_a_x, **kw):
    shared = host_shared(**kw)
    X = np.ascontiguousarray(L0_attn_a_x, dtype=np.float32)
    in_maps = [dict(shared, x=np.ascontiguousarray(X[b])) for b in range(NCORES)]

    if "nc" not in _CACHE:
        _CACHE["nc"] = build_program()
    nc = _CACHE["nc"]
    res = run_bass_kernel_spmd(nc, in_maps, core_ids=list(range(NCORES)))
    out = np.stack([np.asarray(res.results[b]["out"]) for b in range(NCORES)], axis=0)
    return out.astype(np.float32)


# version 21
# speedup vs baseline: 1.1609x; 1.0467x over previous
import math
import numpy as np

import concourse.bass as bass
import concourse.mybir as mybir
from concourse.bass_utils import run_bass_kernel_spmd

F32 = mybir.dt.float32
AF = mybir.ActivationFunctionType
OP = mybir.AluOpType
AX = mybir.AxisListType

NCORES = 8
S = 4096
DM = 1024
TILE = 512
NT = S // TILE
import os as _os
_TEST_NT = int(_os.environ.get('K_NT', '0'))
MSTOP = int(_os.environ.get('K_MSTOP', '0'))
if _TEST_NT:
    NT = _TEST_NT
MT = 128
DI = 2048
DS = 16
NH = 16
HD = 64
FD = 128
N_SUBLAYERS = int(_os.environ.get('K_NSUB', '4'))

ENGINES = ("tensor", "vector", "scalar", "gpsimd", "sync")
NRING = {"sync": 24, "gpsimd": 8}


class Op:
    __slots__ = ("engine", "fn", "args", "kwargs", "reads", "writes", "kind",
                 "deps", "need_inc", "sem", "val", "inc_amt", "pre_wait",
                 "ptr_reads", "late_keys", "pos")

    def __init__(self, engine, fn, args, kwargs, reads, writes, kind):
        self.engine, self.fn, self.args, self.kwargs = engine, fn, args, kwargs
        self.reads, self.writes, self.kind = reads, writes, kind
        self.deps = ()
        self.need_inc = False
        self.sem = None
        self.val = 0
        self.inc_amt = 1
        self.pre_wait = None
        self.ptr_reads = frozenset()
        self.late_keys = frozenset()
        self.pos = 0


class Buf:

    def __init__(self, nc, name, shape, dtype=F32, gran=None, psum=False):
        if psum:
            self.h = nc.alloc_psum_tensor(name, shape, dtype)
        else:
            self.h = nc.alloc_sbuf_tensor(name, shape, dtype)
        self.name = name
        self.cols = shape[-1]
        self.parts = shape[0]
        self.gran = gran if gran else self.cols

    def keys(self, lo, size):
        g = self.gran
        return frozenset(f"{self.name}:{i}" for i in range(lo // g, (lo + size + g - 1) // g))

    def s(self, lo=0, size=None, p0=0, pn=None):
        if size is None:
            size = self.cols - lo
        if pn is None:
            pn = self.parts - p0
        return (self.h[p0:p0 + pn, lo:lo + size], self.keys(lo, size))

    def ap(self, lo=0, size=None, p0=0, pn=None):
        return self.s(lo, size, p0, pn)[0]


class Ctx:
    def __init__(self, nc):
        self.nc = nc
        self.ops = []
        self.eng_sem = {e: nc.alloc_semaphore(f"es_{e}") for e in ENGINES}
        self.dma_sems = {e: [nc.alloc_semaphore(f"ds_{e}_{i}") for i in range(NRING[e])]
                         for e in NRING}
        self.dma_idx = {e: 0 for e in NRING}
        self.eng_pos = {}
        self.store_ops = []

    def op(self, engine, fn, args, kwargs=None, reads=(), writes=(), kind="compute",
           ptr_reads=(), late=()):
        rd, wr, pr, lt = set(), set(), set(), set()
        for x in reads:
            rd |= x
        for x in writes:
            wr |= x
        for x in ptr_reads:
            rd |= x
            pr |= x
        for x in late:
            lt |= x
        o = Op(engine, fn, list(args), dict(kwargs or {}), frozenset(rd), frozenset(wr), kind)
        if kind == "compute" and args and hasattr(args[0], "shape"):
            shp = args[0].shape
            free = 1
            for d_ in shp[1:]:
                free *= d_
            if free <= 16:
                lt |= wr
        o.ptr_reads = frozenset(pr)
        o.late_keys = frozenset(lt)
        o.pos = self.eng_pos.get(engine, 0)
        self.eng_pos[engine] = o.pos + 1
        self.ops.append(o)
        return o

    def load(self, dst, src_ap, engine="sync"):
        dap, dk = dst
        return self.op(engine, "dma_start", [dap, src_ap], {}, [], [dk], kind="dma")

    def store(self, dst_ap, src, engine="gpsimd"):
        sap, sk = src
        o = self.op(engine, "dma_start", [dst_ap, sap], {}, [sk], [], kind="dma")
        self.store_ops.append(o)
        return o

    def mm(self, out, lhsT, rhs, start=True, stop=True):
        oap, ok = out
        lap, lk = lhsT
        rap, rk = rhs
        return self.op("tensor", "matmul", [oap, lap, rap],
                       dict(start=start, stop=stop), [lk, rk], [ok])

    def tr(self, out, in_, ident):
        oap, ok = out
        iap, ik = in_
        return self.op("tensor", "transpose", [oap, iap, ident[0]], {}, [ik, ident[1]], [ok])

    def cp(self, eng, out, in_):
        oap, ok = out
        iap, ik = in_
        fn = "tensor_copy" if eng in ("vector", "gpsimd") else "copy"
        return self.op(eng, fn, [oap, iap], {}, [ik], [ok])

    def tt(self, eng, out, in0, in1, op):
        oap, ok = out
        return self.op(eng, "tensor_tensor", [oap, in0[0], in1[0]], dict(op=op),
                       [in0[1], in1[1]], [ok])

    def ts(self, eng, out, in0, s1, s2, op0, op1=None):
        oap, ok = out
        reads = [in0[1]]
        s1a = s1
        if isinstance(s1, tuple):
            s1a = s1[0]; reads.append(s1[1])
        s2a = s2
        if isinstance(s2, tuple):
            s2a = s2[0]; reads.append(s2[1])
        if op1 is None:
            return self.op(eng, "tensor_scalar", [oap, in0[0], s1a, None, op0], {},
                           [in0[1]], [ok], ptr_reads=reads[1:])
        return self.op(eng, "tensor_scalar", [oap, in0[0], s1a, s2a, op0, op1], {},
                       [in0[1]], [ok], ptr_reads=reads[1:])

    def ts_accum(self, eng, out, in0, s1, op0, accum_out, seed=0.0):
        oap, ok = out
        aap, akk = accum_out
        reads = [in0[1]]
        s1a = s1
        if isinstance(s1, tuple):
            s1a = s1[0]; reads.append(s1[1])
        return self.op(eng, "tensor_scalar", [oap, in0[0], s1a, seed, op0, OP.add],
                       dict(accum_out=aap), [in0[1]], [ok, akk],
                       ptr_reads=reads[1:], late=[akk])

    def stt(self, eng, out, in0, scalar, in1, op0, op1):
        oap, ok = out
        reads = [in0[1], in1[1]]
        sa = scalar
        if isinstance(scalar, tuple):
            sa = scalar[0]; reads.append(scalar[1])
        return self.op(eng, "scalar_tensor_tensor", [oap, in0[0], sa, in1[0], op0, op1],
                       {}, [in0[1], in1[1]], [ok], ptr_reads=reads[2:])

    def act(self, out, in_, func, bias=None, scale=None, accum_out=None):
        oap, ok = out
        reads = [in_[1]]
        preads = []
        kw = {}
        if bias is not None:
            if isinstance(bias, tuple):
                kw["bias"] = bias[0]; preads.append(bias[1])
            else:
                kw["bias"] = bias
        if scale is not None:
            if isinstance(scale, tuple):
                kw["scale"] = scale[0]; preads.append(scale[1])
            else:
                kw["scale"] = scale
        writes = [ok]
        late = []
        if accum_out is not None:
            kw["accum_out"] = accum_out[0]
            writes.append(accum_out[1])
            late.append(accum_out[1])
        return self.op("scalar", "activation", [oap, in_[0], func], kw, reads, writes,
                       ptr_reads=preads, late=late)

    def scan(self, out, d0, d1, init, op0=OP.mult, op1=OP.add):
        oap, ok = out
        return self.op("vector", "tensor_tensor_scan", [oap, d0[0], d1[0], init[0], op0, op1],
                       {}, [d0[1], d1[1]], [ok], ptr_reads=[init[1]], late=[ok])

    def red(self, out, in_, op=OP.add, fn=None):
        oap, ok = out
        if fn:
            return self.op("vector", fn, [oap, in_[0]], dict(axis=AX.X), [in_[1]], [ok],
                           late=[ok])
        return self.op("vector", "tensor_reduce", [oap, in_[0]], dict(axis=AX.X, op=op),
                       [in_[1]], [ok], late=[ok])

    def memset(self, eng, dst, val):
        dap, dk = dst
        return self.op(eng, "memset", [dap, val], {}, [], [dk])

    def barrier(self):
        last = {}
        for i, o in enumerate(self.ops):
            if o.kind != "nop":
                last[o.engine] = i
        for e in ENGINES:
            o = self.op(e, "barrier_wait", [], {}, [], [], kind="nop")
            o.deps = tuple(last[e2] for e2 in ENGINES if e2 != e and e2 in last)
            for j in o.deps:
                self.ops[j].need_inc = True
        return last

    def resolve(self):
        last_w = {}
        readers = {}
        for i, o in enumerate(self.ops):
            if o.kind == "nop":
                last_w.clear()
                readers.clear()
                continue
            dep = set()
            W = 48
            for k in o.reads:
                j = last_w.get(k)
                if j is None:
                    continue
                w = self.ops[j]
                if w.engine != o.engine:
                    dep.add(j)
                elif (k in o.ptr_reads or k in w.late_keys) and (o.pos - w.pos) < W:
                    dep.add(j)
            for k in o.writes:
                j = last_w.get(k)
                if j is not None and self.ops[j].engine != o.engine:
                    dep.add(j)
                for e2, j in readers.get(k, {}).items():
                    if e2 != o.engine:
                        dep.add(j)
            o.deps = tuple(dep)
            for j in dep:
                self.ops[j].need_inc = True
            for k in o.reads:
                readers.setdefault(k, {})[o.engine] = i
            for k in o.writes:
                last_w[k] = i
                readers[k] = {}

        cnt = {e: 0 for e in ENGINES}
        for o in self.ops:
            if o.kind == "dma":
                e = o.engine
                j = self.dma_idx[e]
                nr = NRING[e]
                o.sem = self.dma_sems[e][j % nr]
                o.val = 16 * (j // nr + 1)
                o.inc_amt = 16
                o.need_inc = True
                if j >= nr:
                    o.pre_wait = (self.dma_sems[e][j % nr], 16 * (j // nr))
                self.dma_idx[e] += 1
            elif o.need_inc and o.kind != "nop":
                cnt[o.engine] += 1
                o.sem = self.eng_sem[o.engine]
                o.val = cnt[o.engine]
                o.inc_amt = 1

    def emit(self):
        self.resolve()
        streams = {e: [] for e in ENGINES}
        for o in self.ops:
            streams[o.engine].append(o)
        ops_by_idx = self.ops
        nc = self.nc
        with nc.Block() as block:
            def emit_engine(eng_obj, eng_name):
                waited = {}
                for o in streams[eng_name]:
                    tgt = {}
                    for j in o.deps:
                        d = ops_by_idx[j]
                        assert d.sem is not None, (o.fn, d.fn)
                        key = id(d.sem)
                        if tgt.get(key, (None, -1))[1] < d.val:
                            tgt[key] = (d.sem, d.val)
                    if o.pre_wait is not None:
                        sem, val = o.pre_wait
                        key = id(sem)
                        if tgt.get(key, (None, -1))[1] < val:
                            tgt[key] = (sem, val)
                    for sem, val in tgt.values():
                        if waited.get(id(sem), -1) < val:
                            eng_obj.wait_ge(sem, val)
                            waited[id(sem)] = val
                    if o.kind == "nop":
                        continue
                    ins = getattr(eng_obj, o.fn)(*o.args, **o.kwargs)
                    if o.need_inc:
                        ins.then_inc(o.sem, o.inc_amt)
                if eng_name == "gpsimd":
                    fin = {}
                    for o in self.store_ops:
                        if fin.get(id(o.sem), (None, -1))[1] < o.val:
                            fin[id(o.sem)] = (o.sem, o.val)
                    for sem, val in fin.values():
                        if waited.get(id(sem), -1) < val:
                            eng_obj.wait_ge(sem, val)

            @block.tensor
            def _(t):
                emit_engine(t, "tensor")

            @block.vector
            def _(v):
                emit_engine(v, "vector")

            @block.scalar
            def _(s):
                emit_engine(s, "scalar")

            @block.gpsimd
            def _(g):
                emit_engine(g, "gpsimd")

            @block.sync
            def _(s):
                emit_engine(s, "sync")



def build_program():
    nc = bass.Bass("TRN2", target_bir_lowering=False, detect_race_conditions=False)

    d = {}
    def din(name, shape):
        d[name] = nc.dram_tensor(name, list(shape), F32, kind="ExternalInput")
        return d[name]

    x_d = din("x", (S, DM))
    wq_d = din("wq", (DM, DM)); wk_d = din("wk", (DM, DM))
    wv_d = din("wv", (DM, DM)); wo_d = din("wo", (DM, DM))
    rf2_d = din("rf2", (128, 128))
    gw_d = din("gw", (DM, 8))
    w0T_d = din("w0T", (DM, DM)); w1T_d = din("w1T", (DM, DM))
    b0_d = din("b0r", (1, DM)); b1_d = din("b1r", (1, DM))
    normw_d = din("normw", (128, 8))
    inpT_d = din("inpT", (DM, 2 * DI))
    convw_d = din("convw", (128, 64)); convb_d = din("convb", (128, 16))
    xpT_d = din("xpT", (DI, 96))
    dtpT_d = din("dtpT", (64, DI)); dtb_d = din("dtb", (128, 16))
    alog_d = din("alog", (128, 256)); dvec_d = din("dvec", (128, 16))
    opT_d = din("opT", (DI, DM))
    ident_d = din("ident", (128, 128))
    ones16_d = din("ones16", (16, 128))
    shp_d = din("shp", (128, 1)); b1v_d = din("b1v", (128, 1)); b2v_d = din("b2v", (128, 1))
    epsv_d = din("epsv", (128, 2))
    out_d = nc.dram_tensor("out", [S, DM], F32, kind="ExternalOutput")
    DBG = int(_os.environ.get('K_DBGOUT', '0'))
    dbg_d = {}
    if DBG:
        for nm in ("dKVZ", "dC", "dE", "dF", "dD", "dkp", "dvtm", "dkptm", "dKVT", "dKsum",
                   "pE", "pF", "pC", "pkp", "pkptm", "pvtm", "qC", "qln"):
            dbg_d[nm] = nc.dram_tensor(nm, [128, 4096], F32, kind="ExternalOutput")

    A = Buf(nc, "slotA", [128, 4096], gran=256)
    B = Buf(nc, "slotB", [128, 4096], gran=256)
    C = Buf(nc, "slotC", [128, 4096], gran=256)
    Dd = Buf(nc, "slotD", [128, 4096], gran=256)
    E = Buf(nc, "slotE", [128, 4096], gran=256)
    Fb = Buf(nc, "slotF", [128, 4096], gran=256)
    G = Buf(nc, "slotG", [128, 4096], gran=256)
    H = Buf(nc, "slotH", [128, 4096], gran=256)

    NR_W = 5
    WR = Buf(nc, "wring", [128, NR_W * 1024], gran=1024)

    ident = Buf(nc, "ident_sb", [128, 128])
    rf2 = Buf(nc, "rf2_sb", [128, 128])
    ones16 = Buf(nc, "ones16_sb", [16, 128])
    onescol = Buf(nc, "onescol", [128, 1])
    gwS = Buf(nc, "gw_sb", [128, 64], gran=8)
    b0r = Buf(nc, "b0r_sb", [1, DM])
    b1r = Buf(nc, "b1r_sb", [1, DM])
    normw = Buf(nc, "normw_sb", [128, 8], gran=1)
    convw = Buf(nc, "convw_sb", [128, 64], gran=4)
    convb = Buf(nc, "convb_sb", [128, 16], gran=1)
    dtb = Buf(nc, "dtb_sb", [128, 16], gran=1)
    dvec = Buf(nc, "dvec_sb", [128, 16], gran=1)
    Asb = Buf(nc, "A_sb", [128, 256], gran=16)
    dtpT = Buf(nc, "dtpT_sb", [64, DI], gran=128)
    xpT = Buf(nc, "xpT_sb", [128, 16 * 96], gran=96)
    shp = Buf(nc, "shp_sb", [128, 1])
    b1v = Buf(nc, "b1v_sb", [128, 1])
    b2v = Buf(nc, "b2v_sb", [128, 1])
    KVZ = Buf(nc, "KVZ_sb", [128, NH * 65], gran=65)
    lnv = Buf(nc, "lnv", [128, 24], gran=1)
    kp = Buf(nc, "kp", [128, 512])
    sg1 = Buf(nc, "sg1", [128, 1024], gran=512)
    sg2 = Buf(nc, "sg2", [128, 1024], gran=512)
    kptm = Buf(nc, "kptm", [128, 512], gran=128)
    vtm = Buf(nc, "vtm", [128, 520], gran=65)
    esc = Buf(nc, "esc", [128, 512], gran=128)
    gl = Buf(nc, "gl", [128, 8])
    gl0 = Buf(nc, "gl0", [128, 8])
    gmk = Buf(nc, "gmk", [128, 8])
    ge = Buf(nc, "ge", [128, 8])
    gp = Buf(nc, "gp", [128, 2])
    gv = Buf(nc, "gv", [128, 16], gran=1)
    xdbl = Buf(nc, "xdbl", [128, 192], gran=96)
    dlrfm = Buf(nc, "dlrfm", [64, 256])
    BCfm = Buf(nc, "BCfm", [16, 512], gran=256)
    convs = Buf(nc, "convs", [128, 392], gran=8)
    tmp6 = Buf(nc, "tmp6", [128, 6])
    tails = Buf(nc, "tails", [128, 48], gran=3)
    dAb = Buf(nc, "dAb", [128, 1024], gran=128)
    yv = Buf(nc, "yv", [128, 256])
    state = Buf(nc, "state", [128, 256], gran=16)
    zsc = Buf(nc, "zsc", [128, 128], gran=1)
    brow = Buf(nc, "brow", [1, 512], gran=256)
    epsv = Buf(nc, "epsv_sb", [128, 2], gran=1)

    PS = [Buf(nc, f"ps{i}", [128, 512], psum=True) for i in range(8)]

    ctx = Ctx(nc)
    psum_i = [0]

    def nb():
        b = PS[psum_i[0] % 8]
        psum_i[0] += 1
        return b

    wslot = [0]
    def fetch(dram, r0, c0, ncols=1024):
        sl = wslot[0] % NR_W
        wslot[0] += 1
        dst = WR.s(sl * 1024, ncols)
        ctx.load(dst, dram[r0:r0 + 128, c0:c0 + ncols])
        return dst

    ctx.load(ident.s(), ident_d[:, :])
    ctx.load(rf2.s(), rf2_d[:, :])
    ctx.load(ones16.s(), ones16_d[:, :])
    for ki in range(8):
        ctx.load(gwS.s(ki * 8, 8), gw_d[ki * 128:(ki + 1) * 128, :])
    ctx.load(b0r.s(), b0_d[:, :])
    ctx.load(b1r.s(), b1_d[:, :])
    ctx.load(normw.s(), normw_d[:, :])
    ctx.load(convw.s(), convw_d[:, :])
    ctx.load(convb.s(), convb_d[:, :])
    ctx.load(dtb.s(), dtb_d[:, :])
    ctx.load(dvec.s(), dvec_d[:, :])
    ctx.load(Asb.s(), alog_d[:, :])
    ctx.load(dtpT.s(), dtpT_d[:, :])
    for kj in range(16):
        ctx.load(xpT.s(kj * 96, 96), xpT_d[kj * 128:(kj + 1) * 128, :])
    ctx.load(shp.s(), shp_d[:, :])
    ctx.load(b1v.s(), b1v_d[:, :])
    ctx.load(b2v.s(), b2v_d[:, :])
    ctx.load(epsv.s(), epsv_d[:, :])
    ctx.memset("gpsimd", onescol.s(), 1.0)
    ctx.op("gpsimd", "memset", [bass.AP(vtm.h, 64, [[vtm.cols, 128], [130, 4], [65, 2]]), 1.0],
           {}, [], [vtm.keys(0, 520)])
    ctx.memset("gpsimd", G.s(0, 2048), 0.0)
    ctx.memset("gpsimd", state.s(), 0.0)
    ctx.memset("gpsimd", lnv.s(), 0.0)
    ctx.memset("gpsimd", zsc.s(), 0.0)
    ctx.memset("gpsimd", gv.s(), 0.0)
    ctx.memset("gpsimd", tails.s(), 0.0)
    ctx.act(Asb.s(), Asb.s(), AF.Exp)
    ctx.ts("vector", Asb.s(), Asb.s(), -1.0, None, OP.mult)


    def load_x_tile(T):
        for t_s in range(4):
            ctx.load(A.s(t_s * 1024, 1024),
                     x_d[T * 512 + t_s * 128: T * 512 + (t_s + 1) * 128, :])

    def layer_norm(src, t_s, ln0):
        c_s, c_m, c_q, c_r = ln0, ln0 + 1, ln0 + 2, ln0 + 3
        ctx.red(lnv.s(c_s, 1), src.s(t_s * 1024, 1024), fn="reduce_sum")
        ctx.ts("vector", lnv.s(c_m, 1), lnv.s(c_s, 1), -1.0 / DM, None, OP.mult)
        ctx.ts("vector", C.s(t_s * 1024, 1024), src.s(t_s * 1024, 1024),
               lnv.s(c_m, 1), None, OP.add)
        ctx.act(Dd.s(0, 1024), C.s(t_s * 1024, 1024), AF.Square, accum_out=lnv.s(c_q, 1))
        ctx.act(lnv.s(c_r, 1), lnv.s(c_q, 1), AF.Ln, bias=epsv.s(0, 1), scale=1.0 / DM)
        ctx.act(lnv.s(c_r, 1), lnv.s(c_r, 1), AF.Exp, scale=-0.5)
        ctx.ts("vector", C.s(t_s * 1024, 1024), C.s(t_s * 1024, 1024),
               lnv.s(c_r, 1), None, OP.mult)

    def transpose_tile(src, dst):
        for dc in range(8):
            bank = nb()
            for t_s in range(4):
                ctx.tr((bank.ap(t_s * 128, 128), bank.keys(0, 512)),
                       src.s(t_s * 1024 + dc * 128, 128), ident.s())
            ctx.cp("vector", dst.s(dc * 512, 512), (bank.ap(0, 512), bank.keys(0, 512)))

    def proj1024(w_dram, src_fm, dst_fm, eng_evac="scalar"):
        banks = [nb() for _ in range(8)]
        for ki in range(8):
            wch = fetch(w_dram, ki * 128, 0, 1024)
            for mo in range(8):
                lhsT = (wch[0][:, mo * 128:(mo + 1) * 128], wch[1])
                ctx.mm((banks[mo].ap(), banks[mo].keys(0, 512)), lhsT,
                       src_fm.s(ki * 512, 512), start=(ki == 0), stop=(ki == 7))
        for mo in range(8):
            ctx.cp(eng_evac, dst_fm.s(mo * 512, 512),
                   (banks[mo].ap(), banks[mo].keys(0, 512)))

    def phi_front(src_fm, h):
        mo, poff, par = h // 2, (h % 2) * 64, h % 2
        bank = nb()
        pref = (bank.ap(0, 512), bank.keys(0, 512))
        ctx.mm(pref, rf2.s(0, 128, p0=poff, pn=64),
               src_fm.s(mo * 512, 512, p0=poff, pn=64))
        ctx.act(sg1.s(par * 512, 512), pref, AF.Sign, bias=b1v.s())
        ctx.act(sg2.s(par * 512, 512), pref, AF.Sign, bias=b2v.s(), scale=-1.0)
        return pref

    def phi_back(pref, h, dst):
        par = h % 2
        g1, g2 = sg1.s(par * 512, 512), sg2.s(par * 512, 512)
        ctx.tt("vector", g2, g2, g1, OP.subtract)
        ctx.ts("vector", g2, g2, float(np.pi), None, OP.mult)
        ctx.tt("vector", dst.s(), g2, pref, OP.add)
        ctx.ts("vector", dst.s(0, 512, p0=0, pn=64), dst.s(0, 512, p0=0, pn=64),
               float(np.pi) / 2, None, OP.add)
        CL = float(np.float32(3.1415925))
        ctx.ts("vector", dst.s(), dst.s(), -CL, CL, OP.max, OP.min)
        ctx.act(dst.s(), dst.s(), AF.Sin)

    def ffn_block():
        transpose_tile(C, Dd)
        for t_s in range(4):
            gbank = nb()
            gref = (gbank.ap(0, 8), gbank.keys(0, 512))
            for ki in range(8):
                ctx.mm(gref, Dd.s(ki * 512 + t_s * 128, 128), gwS.s(ki * 8, 8),
                       start=(ki == 0), stop=(ki == 7))
            ctx.cp("vector", gl0.s(), gref)
            ctx.red(gv.s(0, 1), gref, fn="reduce_max")
            ctx.ts("vector", gl.s(), gref, gv.s(0, 1), None, OP.subtract)
            ctx.act(ge.s(), gl.s(), AF.Exp)
            ctx.red(gv.s(1, 1), ge.s(), fn="reduce_sum")
            ctx.op("vector", "reciprocal", [gv.ap(2, 1), gv.ap(1, 1)], {},
                   [gv.keys(1, 1)], [gv.keys(2, 1)], late=[gv.keys(2, 1)])
            ctx.ts("vector", gp.s(), ge.s(0, 2), gv.s(2, 1), None, OP.mult)
            ctx.tt("vector", gv.s(3, 1), gl0.s(0, 1), gv.s(0, 1), OP.is_ge)
            ctx.ts_accum("vector", gmk.s(), gl0.s(), gl0.s(1, 1),
                         OP.is_gt, gv.s(4, 1))
            ctx.tt("vector", gv.s(5, 1), gl0.s(0, 1), gl0.s(1, 1), OP.is_equal)
            ctx.tt("vector", gv.s(6, 1), gv.s(4, 1), gv.s(5, 1), OP.add)
            ctx.op("vector", "tensor_single_scalar",
                   [gv.ap(7, 1), gv.ap(6, 1), 1.0, OP.is_equal], {},
                   [gv.keys(6, 1)], [gv.keys(7, 1)])
            ctx.tt("vector", gv.s(8 + t_s, 1), gp.s(0, 1), gv.s(3, 1), OP.mult)
            ctx.tt("vector", gv.s(12 + t_s, 1), gp.s(1, 1), gv.s(7, 1), OP.mult)
        for ex, (w_dram, brow, ccol0) in enumerate([(w0T_d, b0r, 8), (w1T_d, b1r, 12)]):
            for jb in range(2):
                ebanks = [nb() for _ in range(4)]
                for ki in range(8):
                    wch = fetch(w_dram, ki * 128, jb * 512, 512)
                    for t_s in range(4):
                        ctx.mm((ebanks[t_s].ap(), ebanks[t_s].keys(0, 512)),
                               Dd.s(ki * 512 + t_s * 128, 128), wch,
                               start=(ki == 0), stop=False)
                for t_s in range(4):
                    ctx.mm((ebanks[t_s].ap(), ebanks[t_s].keys(0, 512)),
                           ones16.s(0, 128, p0=0, pn=1),
                           brow.s(jb * 512, 512), start=False, stop=True)
                    eslc = C.s(t_s * 1024 + jb * 512, 512)
                    ctx.act(eslc, (ebanks[t_s].ap(), ebanks[t_s].keys(0, 512)),
                            AF.Silu)
                    ctx.stt("vector", B.s(t_s * 1024 + jb * 512, 512), eslc,
                            gv.s(ccol0 + t_s, 1), B.s(t_s * 1024 + jb * 512, 512),
                            OP.mult, OP.add)

    for T in range(NT):
        load_x_tile(T)
        for t_s in range(4):
            layer_norm(A, t_s, 4 * t_s)
        if DBG and T == 0:
            ctx.store(dbg_d["qC"][:, :], C.s())
            ctx.store(dbg_d["qln"][:, 0:16], lnv.s())
        transpose_tile(C, Dd)
        proj1024(wk_d, Dd, E)
        proj1024(wv_d, Dd, Fb)
        pref_next = phi_front(E, 0)
        for mo in range(8):
            vb = nb()
            for t_s in range(4):
                ctx.tr((vb.ap(t_s * 128, 128), vb.keys(0, 512)),
                       Fb.s(mo * 512 + t_s * 128, 128), ident.s())
            ctx.op("vector", "tensor_copy",
                   [bass.AP(vtm.h, 0, [[vtm.cols, 128], [130, 4], [65, 2], [1, 64]]),
                    bass.AP(vb.h, 0, [[vb.cols, 128], [128, 4], [64, 2], [1, 64]])],
                   {}, [vb.keys(0, 512)], [vtm.keys(0, 520)])
            for hh in range(2):
                h = mo * 2 + hh
                pref = pref_next
                if h + 1 < NH:
                    pref_next = phi_front(E, h + 1)
                phi_back(pref, h, kp)
                kb = nb()
                for t_s in range(4):
                    ctx.tr((kb.ap(t_s * 128, 128), kb.keys(0, 512)),
                           kp.s(t_s * 128, 128), ident.s())
                ctx.cp("vector", kptm.s(0, 512), (kb.ap(0, 512), kb.keys(0, 512)))
                kvb = nb()
                kvref = (kvb.ap(0, 65), kvb.keys(0, 512))
                for t_s in range(4):
                    ctx.mm(kvref, kptm.s(t_s * 128, 128),
                           vtm.s(t_s * 130 + hh * 65, 65),
                           start=(t_s == 0), stop=(t_s == 3))
                ctx.tt("vector", G.s(h * 65, 65), kvref, G.s(h * 65, 65), OP.add)

    ctx.barrier()
    if DBG:
        ctx.store(dbg_d["pE"][:, :], E.s())
        ctx.store(dbg_d["pF"][:, :], Fb.s())
        ctx.store(dbg_d["pC"][:, :], C.s())
        ctx.store(dbg_d["pkp"][:, 0:512], kp.s())
        ctx.store(dbg_d["pkptm"][:, 0:512], kptm.s())
        ctx.store(dbg_d["pvtm"][:, 0:512], vtm.s())
        ctx.barrier()
    ctx.cp("vector", KVZ.s(0, NH * 65), G.s(0, NH * 65))
    ctx.barrier()

    for T in range(NT):
        load_x_tile(T)
        for t_s in range(4):
            layer_norm(A, t_s, 4 * t_s)
        transpose_tile(C, Dd)
        proj1024(wq_d, Dd, E)
        pref_next = phi_front(E, 0)
        for h in range(NH):
            pref = pref_next
            if h + 1 < NH:
                pref_next = phi_front(E, h + 1)
            phi_back(pref, h, kp)
            hq = (h % 4) * 16
            qbs = [nb() for _ in range(4)]
            for t_s in range(4):
                qref = (qbs[t_s].ap(0, 65), qbs[t_s].keys(0, 512))
                ctx.mm(qref, kp.s(t_s * 128, 128), KVZ.s(h * 65, 65))
            for t_s in range(4):
                ctx.ts("vector", zsc.s(hq + t_s, 1),
                       (qbs[t_s].ap(64, 1), qbs[t_s].keys(0, 512)),
                       float(128e-6), None, OP.max)
            ctx.op("vector", "reciprocal", [zsc.ap(hq + 4, 4), zsc.ap(hq, 4)], {},
                   [zsc.keys(hq, 4)], [zsc.keys(hq + 4, 4)], late=[zsc.keys(hq + 4, 4)])
            for t_s in range(4):
                ctx.ts("vector", Fb.s(t_s * 1024 + h * 64, 64),
                       (qbs[t_s].ap(0, 64), qbs[t_s].keys(0, 512)),
                       zsc.s(hq + 4 + t_s, 1), None, OP.mult)
        transpose_tile(Fb, Dd)
        banks = [nb() for _ in range(8)]
        for ki in range(8):
            wch = fetch(wo_d, ki * 128, 0, 1024)
            for i8, (t_s, ob) in enumerate([(a, b) for a in range(4) for b in range(2)]):
                lhsT = Dd.s(ki * 512 + t_s * 128, 128)
                rhs = (wch[0][:, ob * 512:(ob + 1) * 512], wch[1])
                ctx.mm((banks[i8].ap(), banks[i8].keys(0, 512)), lhsT, rhs,
                       start=(ki == 0), stop=(ki == 7))
        for i8, (t_s, ob) in enumerate([(a, b) for a in range(4) for b in range(2)]):
            ctx.tt("vector", B.s(t_s * 1024 + ob * 512, 512),
                   (banks[i8].ap(), banks[i8].keys(0, 512)),
                   A.s(t_s * 1024 + ob * 512, 512), OP.add)

        if N_SUBLAYERS >= 2:
            for t_s in range(4):
                layer_norm(B, t_s, 4 * t_s)
            ffn_block()

        if N_SUBLAYERS >= 3:
            for t_s in range(4):
                layer_norm(B, t_s, 4 * t_s)
                ctx.tt("vector", B.s(t_s * 1024, 1024), B.s(t_s * 1024, 1024),
                       C.s(t_s * 1024, 1024), OP.add)
                ctx.act(Dd.s(0, 1024), C.s(t_s * 1024, 1024), AF.Square,
                        accum_out=lnv.s(16, 1))
                ctx.act(lnv.s(17, 1), lnv.s(16, 1), AF.Ln, bias=epsv.s(0, 1), scale=1.0 / DM)
                ctx.act(lnv.s(17, 1), lnv.s(17, 1), AF.Exp, scale=-0.5)
                ctx.ts("vector", C.s(t_s * 1024, 1024), C.s(t_s * 1024, 1024),
                       lnv.s(17, 1), None, OP.mult)
            transpose_tile(C, Dd)
            for dc8 in range(8):
                ctx.ts("vector", Dd.s(dc8 * 512, 512), Dd.s(dc8 * 512, 512),
                       normw.s(dc8, 1), None, OP.mult)
            def m_front_chunks(mm_i):
                    pp = mm_i % 2
                    hb = pp * 2048
                    chunks = []
                    def in_wave(jg):
                        banks = [nb() for _ in range(8)]
                        for ki in range(8):
                            wch = fetch(inpT_d, ki * 128, jg * 1024, 1024)
                            for jcl in range(8):
                                lhsT = (wch[0][:, jcl * 128:(jcl + 1) * 128], wch[1])
                                ctx.mm((banks[jcl].ap(0, 128), banks[jcl].keys(0, 512)),
                                       lhsT, Dd.s(ki * 512 + mm_i * 128, 128),
                                       start=(ki == 0), stop=(ki == 7))
                        for jcl in range(8):
                            jc = jg * 8 + jcl
                            dstbuf, dcc = (E, jc) if jc < 16 else (Fb, jc - 16)
                            ctx.cp("scalar", dstbuf.s(hb + dcc * 128, 128),
                                   (banks[jcl].ap(0, 128), banks[jcl].keys(0, 512)))
                    for jg in range(4):
                        chunks.append(lambda jg=jg: in_wave(jg))
                    def conv4(g):
                        for dc in range(g * 4, g * 4 + 4):
                            eb0 = hb + dc * 128
                            ctx.cp("vector", convs.s(0, 3), tails.s(dc * 3, 3))
                            ctx.cp("vector", convs.s(3, 128), E.s(eb0, 128))
                            ctx.cp("vector", tails.s(dc * 3, 3), E.s(eb0 + 125, 3))
                            ctx.ts("vector", E.s(eb0, 128), convs.s(0, 128),
                                   convw.s(dc * 4 + 0, 1), convb.s(dc, 1), OP.mult, OP.add)
                            for k in range(1, 4):
                                ctx.stt("vector", E.s(eb0, 128), convs.s(k, 128),
                                        convw.s(dc * 4 + k, 1), E.s(eb0, 128),
                                        OP.mult, OP.add)
                    for g in range(4):
                        chunks.append(lambda g=g: conv4(g))
                    def silu_c():
                        ctx.act(E.s(hb, 2048), E.s(hb, 2048), AF.Silu)
                    chunks.append(silu_c)
                    def xproj_c():
                        xb = nb()
                        xref = (xb.ap(0, 96), xb.keys(0, 512))
                        for kj in range(16):
                            ctx.mm(xref, E.s(hb + kj * 128, 128), xpT.s(kj * 96, 96),
                                   start=(kj == 0), stop=(kj == 15))
                        ctx.cp("vector", xdbl.s(pp * 96, 96), xref)
                    chunks.append(xproj_c)
                    def trans_c():
                        tb = nb()
                        ctx.tr((tb.ap(0, 128, p0=0, pn=64), tb.keys(0, 512)),
                               xdbl.s(pp * 96, 64), ident.s())
                        ctx.cp("vector", dlrfm.s(pp * 128, 128, p0=0, pn=64),
                               (tb.ap(0, 128, p0=0, pn=64), tb.keys(0, 512)))
                        tb2 = nb()
                        ctx.tr((tb2.ap(0, 128, p0=0, pn=16), tb2.keys(0, 512)),
                               xdbl.s(pp * 96 + 64, 16), ident.s())
                        ctx.cp("vector", BCfm.s(pp * 256, 128, p0=0, pn=16),
                               (tb2.ap(0, 128, p0=0, pn=16), tb2.keys(0, 512)))
                        tb3 = nb()
                        ctx.tr((tb3.ap(0, 128, p0=0, pn=16), tb3.keys(0, 512)),
                               xdbl.s(pp * 96 + 80, 16), ident.s())
                        ctx.cp("vector", BCfm.s(pp * 256 + 128, 128, p0=0, pn=16),
                               (tb3.ap(0, 128, p0=0, pn=16), tb3.keys(0, 512)))
                    chunks.append(trans_c)
                    def dt4(g):
                        for dc in range(g * 4, g * 4 + 4):
                            db = nb()
                            dref = (db.ap(0, 128), db.keys(0, 512))
                            ctx.mm(dref, dtpT.s(dc * 128, 128), dlrfm.s(pp * 128, 128))
                            ctx.act(esc.s(pp * 128, 128), dref, AF.Exp, bias=dtb.s(dc, 1))
                            ctx.act(G.s(hb + dc * 128, 128), esc.s(pp * 128, 128), AF.Ln,
                                    bias=epsv.s(1, 1))
                    for g in range(4):
                        chunks.append(lambda g=g: dt4(g))
                    def bc4(g):
                        for n in range(g * 4, g * 4 + 4):
                            xb1 = nb()
                            ctx.mm((xb1.ap(0, 256, p0=0, pn=1), xb1.keys(0, 512)),
                                   ident.s(n, 1, p0=0, pn=16), BCfm.s(pp * 256, 256))
                            ctx.cp("vector", brow.s(pp * 256, 256),
                                   (xb1.ap(0, 256, p0=0, pn=1), xb1.keys(0, 512)))
                            bb = nb()
                            ctx.mm((bb.ap(0, 256), bb.keys(0, 512)),
                                   ones16.s(0, 128, p0=0, pn=1), brow.s(pp * 256, 256))
                            ctx.cp("scalar", C.s(hb + n * 128, 128),
                                   (bb.ap(0, 128), bb.keys(0, 512)))
                            ctx.op("scalar", "copy",
                                   [A.h[:, hb + n: hb + 2048: 16], bb.ap(128, 128)],
                                   {}, [bb.keys(0, 512)], [A.keys(hb, 2048)])
                    for g in range(4):
                        chunks.append(lambda g=g: bc4(g))
                    return chunks

            def m_back_chunks(mm_i):
                pp = mm_i % 2
                hb = pp * 2048
                ho = 2048 - hb

                def du_part(dc):
                    ctx.tt("gpsimd", convs.s(136 + (dc % 2) * 128, 128),
                           G.s(hb + dc * 128, 128), E.s(hb + dc * 128, 128), OP.mult)

                def dbu_half(dc, half):
                    slot = (dc * 2 + half) % 2
                    du_rep = bass.AP(convs.h, 136 + (dc % 2) * 128,
                                     [[convs.cols, 128], [0, 8], [1, 128]])
                    ctx.op("gpsimd", "tensor_tensor",
                           [H.h[:, ho + slot * 1024: ho + slot * 1024 + 1024]
                            .rearrange("p (a b) -> p a b", a=8),
                            du_rep,
                            C.h[:, hb + half * 1024: hb + half * 1024 + 1024]
                            .rearrange("p (a b) -> p a b", a=8),
                            OP.mult],
                           {}, [convs.keys(136 + (dc % 2) * 128, 128),
                                C.keys(hb + half * 1024, 1024)],
                           [H.keys(ho + slot * 1024, 1024)])

                def pre_c():
                    ctx.act(Fb.s(hb, 2048), Fb.s(hb, 2048), AF.Silu)
                    du_part(0)
                    dbu_half(0, 0)

                def dc_c(dc):
                    eb0 = hb + dc * 128
                    for half in range(2):
                        slot = (dc * 2 + half) % 2
                        for j in range(8):
                            n = half * 8 + j
                            ctx.act(dAb.s(j * 128, 128), G.s(eb0, 128), AF.Exp,
                                    scale=Asb.s(dc * 16 + n, 1))
                        for j in range(8):
                            n = half * 8 + j
                            ctx.op("vector", "tensor_tensor_scan",
                                   [H.h[:, hb + n: hb + 2048: 16], dAb.ap(j * 128, 128),
                                    H.h[:, ho + slot * 1024 + j * 128:
                                         ho + slot * 1024 + (j + 1) * 128],
                                    state.ap(dc * 16 + n, 1), OP.mult, OP.add], {},
                                   [dAb.keys(j * 128, 128),
                                    H.keys(ho + slot * 1024 + j * 128, 128)],
                                   [H.keys(hb, 2048)],
                                   ptr_reads=[state.keys(dc * 16 + n, 1)],
                                   late=[H.keys(hb, 2048)])
                        if half == 0:
                            dbu_half(dc, 1)
                        elif dc + 1 < 16:
                            du_part(dc + 1)
                            dbu_half(dc + 1, 0)
                    ctx.cp("vector", state.s(dc * 16, 16), H.s(hb + 2032, 16))
                    ctx.tt("vector", H.s(hb, 2048), H.s(hb, 2048), A.s(hb, 2048), OP.mult)
                    h3 = H.h[:, hb: hb + 2048].rearrange("p (t n) -> p t n", n=16)
                    ctx.op("vector", "tensor_reduce", [yv.ap(0, 128), h3],
                           dict(axis=AX.X, op=OP.add), [H.keys(hb, 2048)],
                           [yv.keys(0, 128)], late=[yv.keys(0, 128)])
                    ctx.stt("vector", yv.s(0, 128), E.s(eb0, 128),
                            dvec.s(dc, 1), yv.s(0, 128), OP.mult, OP.add)
                    ctx.tt("vector", E.s(eb0, 128), yv.s(0, 128),
                           Fb.s(eb0, 128), OP.mult)

                def out_c():
                    banks2 = [nb() for _ in range(2)]
                    for kj in range(16):
                        wch = fetch(opT_d, kj * 128, 0, 1024)
                        for ob in range(2):
                            rhs = (wch[0][:, ob * 512:(ob + 1) * 512], wch[1])
                            ctx.mm((banks2[ob].ap(), banks2[ob].keys(0, 512)),
                                   E.s(hb + kj * 128, 128), rhs,
                                   start=(kj == 0), stop=(kj == 15))
                    for ob in range(2):
                        col = mm_i * 1024 + ob * 512
                        ctx.tt("vector", B.s(col, 512),
                               (banks2[ob].ap(), banks2[ob].keys(0, 512)),
                               B.s(col, 512), OP.add)

                return pre_c, [lambda dc=dc: dc_c(dc) for dc in range(16)], out_c

            for k in range(5):
                F = m_front_chunks(k) if k < 4 else []
                if k == 0:
                    for f in F:
                        f()
                    continue
                pre, dcs, out = m_back_chunks(k - 1)
                pre()
                fi = 0
                for i, b in enumerate(dcs):
                    b()
                    want = ((i + 1) * len(F) + 15) // 16
                    while fi < want:
                        F[fi]()
                        fi += 1
                while fi < len(F):
                    F[fi]()
                    fi += 1
                out()
        if N_SUBLAYERS >= 4:
            for t_s in range(4):
                layer_norm(B, t_s, 4 * t_s)
            ffn_block()

        for t_s in range(4):
            ctx.store(out_d[T * 512 + t_s * 128: T * 512 + (t_s + 1) * 128, :],
                      B.s(t_s * 1024, 1024))
    if DBG:
        ctx.barrier()
        ctx.store(dbg_d["dKVZ"][:, 0:KVZ.cols], KVZ.s())
        ctx.store(dbg_d["dC"][:, :], C.s())
        ctx.store(dbg_d["dE"][:, :], E.s())
        ctx.store(dbg_d["dF"][:, :], Fb.s())
        ctx.store(dbg_d["dD"][:, :], Dd.s())
        ctx.store(dbg_d["dkp"][:, 0:512], kp.s())
        ctx.store(dbg_d["dvtm"][:, 0:512], vtm.s())
        ctx.store(dbg_d["dkptm"][:, 0:512], kptm.s())
        ctx.store(dbg_d["dKVT"][:, 0:2048], G.s(0, 2048))
        ctx.store(dbg_d["dKsum"][:, 0:16], Ksum.s())

    ctx.emit()
    return nc



_CACHE = {}

def host_shared(L0_attn_a_Wq, L0_attn_a_Wk, L0_attn_a_Wv, L0_attn_a_Wo,
                L0_attn_a_rf, L0_attn_b_norm_w, L0_attn_b_in_proj_w, L0_attn_b_conv_w,
                L0_attn_b_conv_b, L0_attn_b_x_proj_w, L0_attn_b_dt_proj_w,
                L0_attn_b_dt_proj_b, L0_attn_b_A_log, L0_attn_b_D, L0_attn_b_out_proj_w,
                L0_ffn_gate_w, L0_ffn_w0, L0_ffn_b0, L0_ffn_w1, L0_ffn_b1, **_ignore):
    f = np.float32
    cont = np.ascontiguousarray

    shp_v = np.zeros((128, 1), f); shp_v[0:64] = np.pi / 2
    pi = np.pi
    shared = {
        "wq": cont(L0_attn_a_Wq, dtype=f), "wk": cont(L0_attn_a_Wk, dtype=f),
        "wv": cont(L0_attn_a_Wv, dtype=f), "wo": cont(L0_attn_a_Wo, dtype=f),
        "rf2": cont(np.concatenate([L0_attn_a_rf, L0_attn_a_rf], axis=0), dtype=f),
        "gw": cont(L0_ffn_gate_w, dtype=f),
        "w0T": cont(L0_ffn_w0.T, dtype=f), "w1T": cont(L0_ffn_w1.T, dtype=f),
        "b0r": cont(L0_ffn_b0.reshape(1, DM), dtype=f),
        "b1r": cont(L0_ffn_b1.reshape(1, DM), dtype=f),
        "normw": cont(L0_attn_b_norm_w.reshape(8, 128).T, dtype=f),
        "inpT": cont(L0_attn_b_in_proj_w.T, dtype=f),
        "convw": cont(L0_attn_b_conv_w[:, 0, :].reshape(16, 128, 4)
                      .transpose(1, 0, 2).reshape(128, 64), dtype=f),
        "convb": cont(L0_attn_b_conv_b.reshape(16, 128).T, dtype=f),
        "xpT": cont(L0_attn_b_x_proj_w.T, dtype=f),
        "dtpT": cont(L0_attn_b_dt_proj_w.T, dtype=f),
        "dtb": cont(L0_attn_b_dt_proj_b.reshape(16, 128).T, dtype=f),
        "alog": cont(np.asarray(L0_attn_b_A_log).reshape(16, 128, 16)
                     .transpose(1, 0, 2).reshape(128, 256), dtype=f),
        "dvec": cont(L0_attn_b_D.reshape(16, 128).T, dtype=f),
        "opT": cont(L0_attn_b_out_proj_w.T, dtype=f),
        "ident": np.eye(128, dtype=f),
        "ones16": np.ones((16, 128), f),
        "shp": shp_v,
        "b1v": cont(shp_v - pi, dtype=f),
        "b2v": cont(-shp_v - pi, dtype=f),
        "epsv": cont(np.broadcast_to(np.array([1e-5, 1.0], f), (128, 2)), dtype=f),
    }
    return shared


def kernel(L0_attn_a_x, **kw):
    shared = host_shared(**kw)
    X = np.ascontiguousarray(L0_attn_a_x, dtype=np.float32)
    in_maps = [dict(shared, x=np.ascontiguousarray(X[b])) for b in range(NCORES)]

    if "nc" not in _CACHE:
        _CACHE["nc"] = build_program()
    nc = _CACHE["nc"]
    res = run_bass_kernel_spmd(nc, in_maps, core_ids=list(range(NCORES)))
    out = np.stack([np.asarray(res.results[b]["out"]) for b in range(NCORES)], axis=0)
    return out.astype(np.float32)


# version 22
# speedup vs baseline: 1.1922x; 1.0269x over previous
import math
import numpy as np

import concourse.bass as bass
import concourse.mybir as mybir
from concourse.bass_utils import run_bass_kernel_spmd

F32 = mybir.dt.float32
AF = mybir.ActivationFunctionType
OP = mybir.AluOpType
AX = mybir.AxisListType

NCORES = 8
S = 4096
DM = 1024
TILE = 512
NT = S // TILE
import os as _os
_TEST_NT = int(_os.environ.get('K_NT', '0'))
MSTOP = int(_os.environ.get('K_MSTOP', '0'))
if _TEST_NT:
    NT = _TEST_NT
MT = 128
DI = 2048
DS = 16
NH = 16
HD = 64
FD = 128
N_SUBLAYERS = int(_os.environ.get('K_NSUB', '4'))

ENGINES = ("tensor", "vector", "scalar", "gpsimd", "sync")
NRING = {"sync": 24, "gpsimd": 8}


class Op:
    __slots__ = ("engine", "fn", "args", "kwargs", "reads", "writes", "kind",
                 "deps", "need_inc", "sem", "val", "inc_amt", "pre_wait",
                 "ptr_reads", "late_keys", "pos")

    def __init__(self, engine, fn, args, kwargs, reads, writes, kind):
        self.engine, self.fn, self.args, self.kwargs = engine, fn, args, kwargs
        self.reads, self.writes, self.kind = reads, writes, kind
        self.deps = ()
        self.need_inc = False
        self.sem = None
        self.val = 0
        self.inc_amt = 1
        self.pre_wait = None
        self.ptr_reads = frozenset()
        self.late_keys = frozenset()
        self.pos = 0


class Buf:

    def __init__(self, nc, name, shape, dtype=F32, gran=None, psum=False):
        if psum:
            self.h = nc.alloc_psum_tensor(name, shape, dtype)
        else:
            self.h = nc.alloc_sbuf_tensor(name, shape, dtype)
        self.name = name
        self.cols = shape[-1]
        self.parts = shape[0]
        self.gran = gran if gran else self.cols

    def keys(self, lo, size):
        g = self.gran
        return frozenset(f"{self.name}:{i}" for i in range(lo // g, (lo + size + g - 1) // g))

    def s(self, lo=0, size=None, p0=0, pn=None):
        if size is None:
            size = self.cols - lo
        if pn is None:
            pn = self.parts - p0
        return (self.h[p0:p0 + pn, lo:lo + size], self.keys(lo, size))

    def ap(self, lo=0, size=None, p0=0, pn=None):
        return self.s(lo, size, p0, pn)[0]


class Ctx:
    def __init__(self, nc):
        self.nc = nc
        self.ops = []
        self.eng_sem = {e: nc.alloc_semaphore(f"es_{e}") for e in ENGINES}
        self.dma_sems = {e: [nc.alloc_semaphore(f"ds_{e}_{i}") for i in range(NRING[e])]
                         for e in NRING}
        self.dma_idx = {e: 0 for e in NRING}
        self.eng_pos = {}
        self.store_ops = []

    def op(self, engine, fn, args, kwargs=None, reads=(), writes=(), kind="compute",
           ptr_reads=(), late=()):
        rd, wr, pr, lt = set(), set(), set(), set()
        for x in reads:
            rd |= x
        for x in writes:
            wr |= x
        for x in ptr_reads:
            rd |= x
            pr |= x
        for x in late:
            lt |= x
        o = Op(engine, fn, list(args), dict(kwargs or {}), frozenset(rd), frozenset(wr), kind)
        if kind == "compute" and args and hasattr(args[0], "shape"):
            shp = args[0].shape
            free = 1
            for d_ in shp[1:]:
                free *= d_
            if free <= 16:
                lt |= wr
        o.ptr_reads = frozenset(pr)
        o.late_keys = frozenset(lt)
        o.pos = self.eng_pos.get(engine, 0)
        self.eng_pos[engine] = o.pos + 1
        self.ops.append(o)
        return o

    def load(self, dst, src_ap, engine="sync"):
        dap, dk = dst
        return self.op(engine, "dma_start", [dap, src_ap], {}, [], [dk], kind="dma")

    def store(self, dst_ap, src, engine="gpsimd"):
        sap, sk = src
        o = self.op(engine, "dma_start", [dst_ap, sap], {}, [sk], [], kind="dma")
        self.store_ops.append(o)
        return o

    def mm(self, out, lhsT, rhs, start=True, stop=True):
        oap, ok = out
        lap, lk = lhsT
        rap, rk = rhs
        return self.op("tensor", "matmul", [oap, lap, rap],
                       dict(start=start, stop=stop), [lk, rk], [ok])

    def tr(self, out, in_, ident):
        oap, ok = out
        iap, ik = in_
        return self.op("tensor", "transpose", [oap, iap, ident[0]], {}, [ik, ident[1]], [ok])

    def cp(self, eng, out, in_):
        oap, ok = out
        iap, ik = in_
        fn = "tensor_copy" if eng in ("vector", "gpsimd") else "copy"
        return self.op(eng, fn, [oap, iap], {}, [ik], [ok])

    def tt(self, eng, out, in0, in1, op):
        oap, ok = out
        return self.op(eng, "tensor_tensor", [oap, in0[0], in1[0]], dict(op=op),
                       [in0[1], in1[1]], [ok])

    def ts(self, eng, out, in0, s1, s2, op0, op1=None):
        oap, ok = out
        reads = [in0[1]]
        s1a = s1
        if isinstance(s1, tuple):
            s1a = s1[0]; reads.append(s1[1])
        s2a = s2
        if isinstance(s2, tuple):
            s2a = s2[0]; reads.append(s2[1])
        if op1 is None:
            return self.op(eng, "tensor_scalar", [oap, in0[0], s1a, None, op0], {},
                           [in0[1]], [ok], ptr_reads=reads[1:])
        return self.op(eng, "tensor_scalar", [oap, in0[0], s1a, s2a, op0, op1], {},
                       [in0[1]], [ok], ptr_reads=reads[1:])

    def ts_accum(self, eng, out, in0, s1, op0, accum_out, seed=0.0):
        oap, ok = out
        aap, akk = accum_out
        reads = [in0[1]]
        s1a = s1
        if isinstance(s1, tuple):
            s1a = s1[0]; reads.append(s1[1])
        return self.op(eng, "tensor_scalar", [oap, in0[0], s1a, seed, op0, OP.add],
                       dict(accum_out=aap), [in0[1]], [ok, akk],
                       ptr_reads=reads[1:], late=[akk])

    def stt(self, eng, out, in0, scalar, in1, op0, op1):
        oap, ok = out
        reads = [in0[1], in1[1]]
        sa = scalar
        if isinstance(scalar, tuple):
            sa = scalar[0]; reads.append(scalar[1])
        return self.op(eng, "scalar_tensor_tensor", [oap, in0[0], sa, in1[0], op0, op1],
                       {}, [in0[1], in1[1]], [ok], ptr_reads=reads[2:])

    def act(self, out, in_, func, bias=None, scale=None, accum_out=None):
        oap, ok = out
        reads = [in_[1]]
        preads = []
        kw = {}
        if bias is not None:
            if isinstance(bias, tuple):
                kw["bias"] = bias[0]; preads.append(bias[1])
            else:
                kw["bias"] = bias
        if scale is not None:
            if isinstance(scale, tuple):
                kw["scale"] = scale[0]; preads.append(scale[1])
            else:
                kw["scale"] = scale
        writes = [ok]
        late = []
        if accum_out is not None:
            kw["accum_out"] = accum_out[0]
            writes.append(accum_out[1])
            late.append(accum_out[1])
        return self.op("scalar", "activation", [oap, in_[0], func], kw, reads, writes,
                       ptr_reads=preads, late=late)

    def scan(self, out, d0, d1, init, op0=OP.mult, op1=OP.add):
        oap, ok = out
        return self.op("vector", "tensor_tensor_scan", [oap, d0[0], d1[0], init[0], op0, op1],
                       {}, [d0[1], d1[1]], [ok], ptr_reads=[init[1]], late=[ok])

    def red(self, out, in_, op=OP.add, fn=None):
        oap, ok = out
        if fn:
            return self.op("vector", fn, [oap, in_[0]], dict(axis=AX.X), [in_[1]], [ok],
                           late=[ok])
        return self.op("vector", "tensor_reduce", [oap, in_[0]], dict(axis=AX.X, op=op),
                       [in_[1]], [ok], late=[ok])

    def memset(self, eng, dst, val):
        dap, dk = dst
        return self.op(eng, "memset", [dap, val], {}, [], [dk])

    def barrier(self):
        last = {}
        for i, o in enumerate(self.ops):
            if o.kind != "nop":
                last[o.engine] = i
        for e in ENGINES:
            o = self.op(e, "barrier_wait", [], {}, [], [], kind="nop")
            o.deps = tuple(last[e2] for e2 in ENGINES if e2 != e and e2 in last)
            for j in o.deps:
                self.ops[j].need_inc = True
        return last

    def resolve(self):
        last_w = {}
        readers = {}
        for i, o in enumerate(self.ops):
            if o.kind == "nop":
                last_w.clear()
                readers.clear()
                continue
            dep = set()
            W = 48
            for k in o.reads:
                j = last_w.get(k)
                if j is None:
                    continue
                w = self.ops[j]
                if w.engine != o.engine:
                    dep.add(j)
                elif (k in o.ptr_reads or k in w.late_keys) and (o.pos - w.pos) < W:
                    dep.add(j)
            for k in o.writes:
                j = last_w.get(k)
                if j is not None and self.ops[j].engine != o.engine:
                    dep.add(j)
                for e2, j in readers.get(k, {}).items():
                    if e2 != o.engine:
                        dep.add(j)
            o.deps = tuple(dep)
            for j in dep:
                self.ops[j].need_inc = True
            for k in o.reads:
                readers.setdefault(k, {})[o.engine] = i
            for k in o.writes:
                last_w[k] = i
                readers[k] = {}

        cnt = {e: 0 for e in ENGINES}
        for o in self.ops:
            if o.kind == "dma":
                e = o.engine
                j = self.dma_idx[e]
                nr = NRING[e]
                o.sem = self.dma_sems[e][j % nr]
                o.val = 16 * (j // nr + 1)
                o.inc_amt = 16
                o.need_inc = True
                if j >= nr:
                    o.pre_wait = (self.dma_sems[e][j % nr], 16 * (j // nr))
                self.dma_idx[e] += 1
            elif o.need_inc and o.kind != "nop":
                cnt[o.engine] += 1
                o.sem = self.eng_sem[o.engine]
                o.val = cnt[o.engine]
                o.inc_amt = 1

    def emit(self):
        self.resolve()
        streams = {e: [] for e in ENGINES}
        for o in self.ops:
            streams[o.engine].append(o)
        ops_by_idx = self.ops
        nc = self.nc
        with nc.Block() as block:
            def emit_engine(eng_obj, eng_name):
                waited = {}
                for o in streams[eng_name]:
                    tgt = {}
                    for j in o.deps:
                        d = ops_by_idx[j]
                        assert d.sem is not None, (o.fn, d.fn)
                        key = id(d.sem)
                        if tgt.get(key, (None, -1))[1] < d.val:
                            tgt[key] = (d.sem, d.val)
                    if o.pre_wait is not None:
                        sem, val = o.pre_wait
                        key = id(sem)
                        if tgt.get(key, (None, -1))[1] < val:
                            tgt[key] = (sem, val)
                    for sem, val in tgt.values():
                        if waited.get(id(sem), -1) < val:
                            eng_obj.wait_ge(sem, val)
                            waited[id(sem)] = val
                    if o.kind == "nop":
                        continue
                    ins = getattr(eng_obj, o.fn)(*o.args, **o.kwargs)
                    if o.need_inc:
                        ins.then_inc(o.sem, o.inc_amt)
                if eng_name == "gpsimd":
                    fin = {}
                    for o in self.store_ops:
                        if fin.get(id(o.sem), (None, -1))[1] < o.val:
                            fin[id(o.sem)] = (o.sem, o.val)
                    for sem, val in fin.values():
                        if waited.get(id(sem), -1) < val:
                            eng_obj.wait_ge(sem, val)

            @block.tensor
            def _(t):
                emit_engine(t, "tensor")

            @block.vector
            def _(v):
                emit_engine(v, "vector")

            @block.scalar
            def _(s):
                emit_engine(s, "scalar")

            @block.gpsimd
            def _(g):
                emit_engine(g, "gpsimd")

            @block.sync
            def _(s):
                emit_engine(s, "sync")



def build_program():
    nc = bass.Bass("TRN2", target_bir_lowering=False, detect_race_conditions=False)

    d = {}
    def din(name, shape):
        d[name] = nc.dram_tensor(name, list(shape), F32, kind="ExternalInput")
        return d[name]

    x_d = din("x", (S, DM))
    wq_d = din("wq", (DM, DM)); wk_d = din("wk", (DM, DM))
    wv_d = din("wv", (DM, DM)); wo_d = din("wo", (DM, DM))
    rf2_d = din("rf2", (128, 128))
    gw_d = din("gw", (DM, 8))
    w0T_d = din("w0T", (DM, DM)); w1T_d = din("w1T", (DM, DM))
    b0_d = din("b0r", (1, DM)); b1_d = din("b1r", (1, DM))
    normw_d = din("normw", (128, 8))
    inpT_d = din("inpT", (DM, 2 * DI))
    convw_d = din("convw", (128, 64)); convb_d = din("convb", (128, 16))
    xpT_d = din("xpT", (DI, 96))
    dtpT_d = din("dtpT", (64, DI)); dtb_d = din("dtb", (128, 16))
    alog_d = din("alog", (128, 256)); dvec_d = din("dvec", (128, 16))
    opT_d = din("opT", (DI, DM))
    ident_d = din("ident", (128, 128))
    ones16_d = din("ones16", (16, 128))
    shp_d = din("shp", (128, 1)); b1v_d = din("b1v", (128, 1)); b2v_d = din("b2v", (128, 1))
    epsv_d = din("epsv", (128, 2))
    out_d = nc.dram_tensor("out", [S, DM], F32, kind="ExternalOutput")
    DBG = int(_os.environ.get('K_DBGOUT', '0'))
    dbg_d = {}
    if DBG:
        for nm in ("dKVZ", "dC", "dE", "dF", "dD", "dkp", "dvtm", "dkptm", "dKVT", "dKsum",
                   "pE", "pF", "pC", "pkp", "pkptm", "pvtm", "qC", "qln"):
            dbg_d[nm] = nc.dram_tensor(nm, [128, 4096], F32, kind="ExternalOutput")

    A = Buf(nc, "slotA", [128, 4096], gran=256)
    B = Buf(nc, "slotB", [128, 4096], gran=256)
    C = Buf(nc, "slotC", [128, 4096], gran=256)
    Dd = Buf(nc, "slotD", [128, 4096], gran=256)
    E = Buf(nc, "slotE", [128, 4096], gran=256)
    Fb = Buf(nc, "slotF", [128, 4096], gran=256)
    G = Buf(nc, "slotG", [128, 4096], gran=256)
    H = Buf(nc, "slotH", [128, 4096], gran=256)

    NR_W = 5
    WR = Buf(nc, "wring", [128, NR_W * 1024], gran=1024)

    ident = Buf(nc, "ident_sb", [128, 128])
    rf2 = Buf(nc, "rf2_sb", [128, 128])
    ones16 = Buf(nc, "ones16_sb", [16, 128])
    onescol = Buf(nc, "onescol", [128, 1])
    gwS = Buf(nc, "gw_sb", [128, 64], gran=8)
    b0r = Buf(nc, "b0r_sb", [1, DM])
    b1r = Buf(nc, "b1r_sb", [1, DM])
    normw = Buf(nc, "normw_sb", [128, 8], gran=1)
    convw = Buf(nc, "convw_sb", [128, 64], gran=4)
    convb = Buf(nc, "convb_sb", [128, 16], gran=1)
    dtb = Buf(nc, "dtb_sb", [128, 16], gran=1)
    dvec = Buf(nc, "dvec_sb", [128, 16], gran=1)
    Asb = Buf(nc, "A_sb", [128, 256], gran=16)
    dtpT = Buf(nc, "dtpT_sb", [64, DI], gran=128)
    xpT = Buf(nc, "xpT_sb", [128, 16 * 96], gran=96)
    shp = Buf(nc, "shp_sb", [128, 1])
    b1v = Buf(nc, "b1v_sb", [128, 1])
    b2v = Buf(nc, "b2v_sb", [128, 1])
    KVZ = Buf(nc, "KVZ_sb", [128, NH * 65], gran=65)
    lnv = Buf(nc, "lnv", [128, 24], gran=1)
    kp = Buf(nc, "kp", [128, 512])
    sg1 = Buf(nc, "sg1", [128, 1024], gran=512)
    sg2 = Buf(nc, "sg2", [128, 1024], gran=512)
    kptm = Buf(nc, "kptm", [128, 512], gran=128)
    vtm = Buf(nc, "vtm", [128, 520], gran=65)
    esc = Buf(nc, "esc", [128, 512], gran=128)
    gl = Buf(nc, "gl", [128, 8])
    gl0 = Buf(nc, "gl0", [128, 8])
    gmk = Buf(nc, "gmk", [128, 8])
    ge = Buf(nc, "ge", [128, 8])
    gp = Buf(nc, "gp", [128, 2])
    gv = Buf(nc, "gv", [128, 16], gran=1)
    xdbl = Buf(nc, "xdbl", [128, 192], gran=96)
    dlrfm = Buf(nc, "dlrfm", [64, 256])
    BCfm = Buf(nc, "BCfm", [16, 512], gran=256)
    convs = Buf(nc, "convs", [128, 392], gran=8)
    tmp6 = Buf(nc, "tmp6", [128, 6])
    tails = Buf(nc, "tails", [128, 48], gran=3)
    dAb = Buf(nc, "dAb", [128, 1024], gran=128)
    yv = Buf(nc, "yv", [128, 256])
    state = Buf(nc, "state", [128, 256], gran=16)
    zsc = Buf(nc, "zsc", [128, 128], gran=1)
    brow = Buf(nc, "brow", [1, 512], gran=256)
    epsv = Buf(nc, "epsv_sb", [128, 2], gran=1)

    PS = [Buf(nc, f"ps{i}", [128, 512], psum=True) for i in range(8)]

    ctx = Ctx(nc)
    psum_i = [0]

    def nb():
        b = PS[psum_i[0] % 8]
        psum_i[0] += 1
        return b

    wslot = [0]
    def fetch(dram, r0, c0, ncols=1024):
        sl = wslot[0] % NR_W
        wslot[0] += 1
        dst = WR.s(sl * 1024, ncols)
        ctx.load(dst, dram[r0:r0 + 128, c0:c0 + ncols])
        return dst

    ctx.load(ident.s(), ident_d[:, :])
    ctx.load(rf2.s(), rf2_d[:, :])
    ctx.load(ones16.s(), ones16_d[:, :])
    for ki in range(8):
        ctx.load(gwS.s(ki * 8, 8), gw_d[ki * 128:(ki + 1) * 128, :])
    ctx.load(b0r.s(), b0_d[:, :])
    ctx.load(b1r.s(), b1_d[:, :])
    ctx.load(normw.s(), normw_d[:, :])
    ctx.load(convw.s(), convw_d[:, :])
    ctx.load(convb.s(), convb_d[:, :])
    ctx.load(dtb.s(), dtb_d[:, :])
    ctx.load(dvec.s(), dvec_d[:, :])
    ctx.load(Asb.s(), alog_d[:, :])
    ctx.load(dtpT.s(), dtpT_d[:, :])
    for kj in range(16):
        ctx.load(xpT.s(kj * 96, 96), xpT_d[kj * 128:(kj + 1) * 128, :])
    ctx.load(shp.s(), shp_d[:, :])
    ctx.load(b1v.s(), b1v_d[:, :])
    ctx.load(b2v.s(), b2v_d[:, :])
    ctx.load(epsv.s(), epsv_d[:, :])
    ctx.memset("gpsimd", onescol.s(), 1.0)
    ctx.op("gpsimd", "memset", [bass.AP(vtm.h, 64, [[vtm.cols, 128], [130, 4], [65, 2]]), 1.0],
           {}, [], [vtm.keys(0, 520)])
    ctx.memset("gpsimd", G.s(0, 2048), 0.0)
    ctx.memset("gpsimd", state.s(), 0.0)
    ctx.memset("gpsimd", lnv.s(), 0.0)
    ctx.memset("gpsimd", zsc.s(), 0.0)
    ctx.memset("gpsimd", gv.s(), 0.0)
    ctx.memset("gpsimd", tails.s(), 0.0)
    ctx.act(Asb.s(), Asb.s(), AF.Exp)
    ctx.ts("vector", Asb.s(), Asb.s(), -1.0, None, OP.mult)


    def load_x_tile(T):
        for t_s in range(4):
            ctx.load(A.s(t_s * 1024, 1024),
                     x_d[T * 512 + t_s * 128: T * 512 + (t_s + 1) * 128, :])

    def layer_norm(src, t_s, ln0, scr=None):
        c_s, c_m, c_q, c_r = ln0, ln0 + 1, ln0 + 2, ln0 + 3
        sq = scr if scr is not None else Dd.s(0, 1024)
        ctx.red(lnv.s(c_s, 1), src.s(t_s * 1024, 1024), fn="reduce_sum")
        ctx.ts("vector", lnv.s(c_m, 1), lnv.s(c_s, 1), -1.0 / DM, None, OP.mult)
        ctx.ts("vector", C.s(t_s * 1024, 1024), src.s(t_s * 1024, 1024),
               lnv.s(c_m, 1), None, OP.add)
        ctx.act(sq, C.s(t_s * 1024, 1024), AF.Square, accum_out=lnv.s(c_q, 1))
        ctx.act(lnv.s(c_r, 1), lnv.s(c_q, 1), AF.Ln, bias=epsv.s(0, 1), scale=1.0 / DM)
        ctx.act(lnv.s(c_r, 1), lnv.s(c_r, 1), AF.Exp, scale=-0.5)
        ctx.ts("vector", C.s(t_s * 1024, 1024), C.s(t_s * 1024, 1024),
               lnv.s(c_r, 1), None, OP.mult)

    def transpose_tile(src, dst):
        for dc in range(8):
            bank = nb()
            for t_s in range(4):
                ctx.tr((bank.ap(t_s * 128, 128), bank.keys(0, 512)),
                       src.s(t_s * 1024 + dc * 128, 128), ident.s())
            ctx.cp("vector", dst.s(dc * 512, 512), (bank.ap(0, 512), bank.keys(0, 512)))

    def proj1024(w_dram, src_fm, dst_fm, eng_evac="scalar"):
        banks = [nb() for _ in range(8)]
        for ki in range(8):
            wch = fetch(w_dram, ki * 128, 0, 1024)
            for mo in range(8):
                lhsT = (wch[0][:, mo * 128:(mo + 1) * 128], wch[1])
                ctx.mm((banks[mo].ap(), banks[mo].keys(0, 512)), lhsT,
                       src_fm.s(ki * 512, 512), start=(ki == 0), stop=(ki == 7))
        for mo in range(8):
            ctx.cp(eng_evac, dst_fm.s(mo * 512, 512),
                   (banks[mo].ap(), banks[mo].keys(0, 512)))

    def phi_front(src_fm, h):
        mo, poff, par = h // 2, (h % 2) * 64, h % 2
        bank = nb()
        pref = (bank.ap(0, 512), bank.keys(0, 512))
        ctx.mm(pref, rf2.s(0, 128, p0=poff, pn=64),
               src_fm.s(mo * 512, 512, p0=poff, pn=64))
        ctx.act(sg1.s(par * 512, 512), pref, AF.Sign, bias=b1v.s())
        ctx.act(sg2.s(par * 512, 512), pref, AF.Sign, bias=b2v.s(), scale=-1.0)
        return pref

    def phi_back(pref, h, dst):
        par = h % 2
        g1, g2 = sg1.s(par * 512, 512), sg2.s(par * 512, 512)
        ctx.tt("vector", g2, g2, g1, OP.subtract)
        ctx.ts("vector", g2, g2, float(np.pi), None, OP.mult)
        ctx.tt("vector", dst.s(), g2, pref, OP.add)
        ctx.ts("vector", dst.s(0, 512, p0=0, pn=64), dst.s(0, 512, p0=0, pn=64),
               float(np.pi) / 2, None, OP.add)
        CL = float(np.float32(3.1415925))
        ctx.ts("vector", dst.s(), dst.s(), -CL, CL, OP.max, OP.min)
        ctx.act(dst.s(), dst.s(), AF.Sin)

    def ffn_tr(dcs, ts_list):
        t0 = ts_list[0]
        for dc in dcs:
            bank = nb()
            for i, t_s in enumerate(ts_list):
                ctx.tr((bank.ap(i * 128, 128), bank.keys(0, 512)),
                       C.s(t_s * 1024 + dc * 128, 128), ident.s())
            ctx.cp("vector", Dd.s(dc * 512 + t0 * 128, len(ts_list) * 128),
                   (bank.ap(0, len(ts_list) * 128), bank.keys(0, 512)))

    def ffn_gate(t_s):
        gbank = nb()
        gref = (gbank.ap(0, 8), gbank.keys(0, 512))
        for ki in range(8):
            ctx.mm(gref, Dd.s(ki * 512 + t_s * 128, 128), gwS.s(ki * 8, 8),
                   start=(ki == 0), stop=(ki == 7))
        ctx.cp("vector", gl0.s(), gref)
        ctx.red(gv.s(0, 1), gref, fn="reduce_max")
        ctx.ts("vector", gl.s(), gref, gv.s(0, 1), None, OP.subtract)
        ctx.act(ge.s(), gl.s(), AF.Exp)
        ctx.red(gv.s(1, 1), ge.s(), fn="reduce_sum")
        ctx.op("vector", "reciprocal", [gv.ap(2, 1), gv.ap(1, 1)], {},
               [gv.keys(1, 1)], [gv.keys(2, 1)], late=[gv.keys(2, 1)])
        ctx.ts("vector", gp.s(), ge.s(0, 2), gv.s(2, 1), None, OP.mult)
        ctx.tt("vector", gv.s(3, 1), gl0.s(0, 1), gv.s(0, 1), OP.is_ge)
        ctx.ts_accum("vector", gmk.s(), gl0.s(), gl0.s(1, 1),
                     OP.is_gt, gv.s(4, 1))
        ctx.tt("vector", gv.s(5, 1), gl0.s(0, 1), gl0.s(1, 1), OP.is_equal)
        ctx.tt("vector", gv.s(6, 1), gv.s(4, 1), gv.s(5, 1), OP.add)
        ctx.op("vector", "tensor_single_scalar",
               [gv.ap(7, 1), gv.ap(6, 1), 1.0, OP.is_equal], {},
               [gv.keys(6, 1)], [gv.keys(7, 1)])
        ctx.tt("vector", gv.s(8 + t_s, 1), gp.s(0, 1), gv.s(3, 1), OP.mult)
        ctx.tt("vector", gv.s(12 + t_s, 1), gp.s(1, 1), gv.s(7, 1), OP.mult)

    def ffn_expert(ex, jb, ts_list):
        w_dram, br_, ccol0 = [(w0T_d, b0r, 8), (w1T_d, b1r, 12)][ex]
        ebanks = {t_s: nb() for t_s in ts_list}
        for ki in range(8):
            wch = fetch(w_dram, ki * 128, jb * 512, 512)
            for t_s in ts_list:
                ctx.mm((ebanks[t_s].ap(), ebanks[t_s].keys(0, 512)),
                       Dd.s(ki * 512 + t_s * 128, 128), wch,
                       start=(ki == 0), stop=False)
        for t_s in ts_list:
            ctx.mm((ebanks[t_s].ap(), ebanks[t_s].keys(0, 512)),
                   ones16.s(0, 128, p0=0, pn=1),
                   br_.s(jb * 512, 512), start=False, stop=True)
            eslc = C.s(t_s * 1024 + jb * 512, 512)
            ctx.act(eslc, (ebanks[t_s].ap(), ebanks[t_s].keys(0, 512)),
                    AF.Silu)
            ctx.stt("vector", B.s(t_s * 1024 + jb * 512, 512), eslc,
                    gv.s(ccol0 + t_s, 1), B.s(t_s * 1024 + jb * 512, 512),
                    OP.mult, OP.add)

    def ffn_block():
        ffn_tr(range(8), [0, 1, 2, 3])
        for t_s in range(4):
            ffn_gate(t_s)
        for ex in range(2):
            for jb in range(2):
                ffn_expert(ex, jb, [0, 1, 2, 3])

    for T in range(NT):
        load_x_tile(T)
        for t_s in range(4):
            layer_norm(A, t_s, 4 * t_s)
        if DBG and T == 0:
            ctx.store(dbg_d["qC"][:, :], C.s())
            ctx.store(dbg_d["qln"][:, 0:16], lnv.s())
        transpose_tile(C, Dd)
        proj1024(wk_d, Dd, E)
        proj1024(wv_d, Dd, Fb)
        pref_next = phi_front(E, 0)
        for mo in range(8):
            vb = nb()
            for t_s in range(4):
                ctx.tr((vb.ap(t_s * 128, 128), vb.keys(0, 512)),
                       Fb.s(mo * 512 + t_s * 128, 128), ident.s())
            ctx.op("vector", "tensor_copy",
                   [bass.AP(vtm.h, 0, [[vtm.cols, 128], [130, 4], [65, 2], [1, 64]]),
                    bass.AP(vb.h, 0, [[vb.cols, 128], [128, 4], [64, 2], [1, 64]])],
                   {}, [vb.keys(0, 512)], [vtm.keys(0, 520)])
            for hh in range(2):
                h = mo * 2 + hh
                pref = pref_next
                if h + 1 < NH:
                    pref_next = phi_front(E, h + 1)
                phi_back(pref, h, kp)
                kb = nb()
                for t_s in range(4):
                    ctx.tr((kb.ap(t_s * 128, 128), kb.keys(0, 512)),
                           kp.s(t_s * 128, 128), ident.s())
                ctx.cp("vector", kptm.s(0, 512), (kb.ap(0, 512), kb.keys(0, 512)))
                kvb = nb()
                kvref = (kvb.ap(0, 65), kvb.keys(0, 512))
                for t_s in range(4):
                    ctx.mm(kvref, kptm.s(t_s * 128, 128),
                           vtm.s(t_s * 130 + hh * 65, 65),
                           start=(t_s == 0), stop=(t_s == 3))
                ctx.tt("vector", G.s(h * 65, 65), kvref, G.s(h * 65, 65), OP.add)

    ctx.barrier()
    if DBG:
        ctx.store(dbg_d["pE"][:, :], E.s())
        ctx.store(dbg_d["pF"][:, :], Fb.s())
        ctx.store(dbg_d["pC"][:, :], C.s())
        ctx.store(dbg_d["pkp"][:, 0:512], kp.s())
        ctx.store(dbg_d["pkptm"][:, 0:512], kptm.s())
        ctx.store(dbg_d["pvtm"][:, 0:512], vtm.s())
        ctx.barrier()
    ctx.cp("vector", KVZ.s(0, NH * 65), G.s(0, NH * 65))
    ctx.barrier()

    for T in range(NT):
        load_x_tile(T)
        for t_s in range(4):
            layer_norm(A, t_s, 4 * t_s)
        transpose_tile(C, Dd)
        proj1024(wq_d, Dd, E)
        pref_next = phi_front(E, 0)
        for h in range(NH):
            pref = pref_next
            if h + 1 < NH:
                pref_next = phi_front(E, h + 1)
            phi_back(pref, h, kp)
            hq = (h % 4) * 16
            qbs = [nb() for _ in range(4)]
            for t_s in range(4):
                qref = (qbs[t_s].ap(0, 65), qbs[t_s].keys(0, 512))
                ctx.mm(qref, kp.s(t_s * 128, 128), KVZ.s(h * 65, 65))
            for t_s in range(4):
                ctx.ts("vector", zsc.s(hq + t_s, 1),
                       (qbs[t_s].ap(64, 1), qbs[t_s].keys(0, 512)),
                       float(128e-6), None, OP.max)
            ctx.op("vector", "reciprocal", [zsc.ap(hq + 4, 4), zsc.ap(hq, 4)], {},
                   [zsc.keys(hq, 4)], [zsc.keys(hq + 4, 4)], late=[zsc.keys(hq + 4, 4)])
            for t_s in range(4):
                ctx.ts("vector", Fb.s(t_s * 1024 + h * 64, 64),
                       (qbs[t_s].ap(0, 64), qbs[t_s].keys(0, 512)),
                       zsc.s(hq + 4 + t_s, 1), None, OP.mult)
        transpose_tile(Fb, Dd)
        banks = [nb() for _ in range(8)]
        for ki in range(8):
            wch = fetch(wo_d, ki * 128, 0, 1024)
            for i8, (t_s, ob) in enumerate([(a, b) for a in range(4) for b in range(2)]):
                lhsT = Dd.s(ki * 512 + t_s * 128, 128)
                rhs = (wch[0][:, ob * 512:(ob + 1) * 512], wch[1])
                ctx.mm((banks[i8].ap(), banks[i8].keys(0, 512)), lhsT, rhs,
                       start=(ki == 0), stop=(ki == 7))
        for i8, (t_s, ob) in enumerate([(a, b) for a in range(4) for b in range(2)]):
            ctx.tt("vector", B.s(t_s * 1024 + ob * 512, 512),
                   (banks[i8].ap(), banks[i8].keys(0, 512)),
                   A.s(t_s * 1024 + ob * 512, 512), OP.add)

        if N_SUBLAYERS >= 2:
            for t_s in range(4):
                layer_norm(B, t_s, 4 * t_s)
            ffn_block()

        if N_SUBLAYERS >= 3:
            for t_s in range(4):
                layer_norm(B, t_s, 4 * t_s)
                ctx.tt("vector", B.s(t_s * 1024, 1024), B.s(t_s * 1024, 1024),
                       C.s(t_s * 1024, 1024), OP.add)
                ctx.act(Dd.s(0, 1024), C.s(t_s * 1024, 1024), AF.Square,
                        accum_out=lnv.s(16, 1))
                ctx.act(lnv.s(17, 1), lnv.s(16, 1), AF.Ln, bias=epsv.s(0, 1), scale=1.0 / DM)
                ctx.act(lnv.s(17, 1), lnv.s(17, 1), AF.Exp, scale=-0.5)
                ctx.ts("vector", C.s(t_s * 1024, 1024), C.s(t_s * 1024, 1024),
                       lnv.s(17, 1), None, OP.mult)
            transpose_tile(C, Dd)
            for dc8 in range(8):
                ctx.ts("vector", Dd.s(dc8 * 512, 512), Dd.s(dc8 * 512, 512),
                       normw.s(dc8, 1), None, OP.mult)
            def m_front_chunks(mm_i):
                    pp = mm_i % 2
                    hb = pp * 2048
                    chunks = []
                    def in_wave(jg):
                        banks = [nb() for _ in range(8)]
                        for ki in range(8):
                            wch = fetch(inpT_d, ki * 128, jg * 1024, 1024)
                            for jcl in range(8):
                                lhsT = (wch[0][:, jcl * 128:(jcl + 1) * 128], wch[1])
                                ctx.mm((banks[jcl].ap(0, 128), banks[jcl].keys(0, 512)),
                                       lhsT, Dd.s(ki * 512 + mm_i * 128, 128),
                                       start=(ki == 0), stop=(ki == 7))
                        for jcl in range(8):
                            jc = jg * 8 + jcl
                            dstbuf, dcc = (E, jc) if jc < 16 else (Fb, jc - 16)
                            ctx.cp("scalar", dstbuf.s(hb + dcc * 128, 128),
                                   (banks[jcl].ap(0, 128), banks[jcl].keys(0, 512)))
                    for jg in range(4):
                        chunks.append(lambda jg=jg: in_wave(jg))
                    def conv4(g):
                        for dc in range(g * 4, g * 4 + 4):
                            eb0 = hb + dc * 128
                            ctx.cp("vector", convs.s(0, 3), tails.s(dc * 3, 3))
                            ctx.cp("vector", convs.s(3, 128), E.s(eb0, 128))
                            ctx.cp("vector", tails.s(dc * 3, 3), E.s(eb0 + 125, 3))
                            ctx.ts("vector", E.s(eb0, 128), convs.s(0, 128),
                                   convw.s(dc * 4 + 0, 1), convb.s(dc, 1), OP.mult, OP.add)
                            for k in range(1, 4):
                                ctx.stt("vector", E.s(eb0, 128), convs.s(k, 128),
                                        convw.s(dc * 4 + k, 1), E.s(eb0, 128),
                                        OP.mult, OP.add)
                    for g in range(4):
                        chunks.append(lambda g=g: conv4(g))
                    def silu_c():
                        ctx.act(E.s(hb, 2048), E.s(hb, 2048), AF.Silu)
                    chunks.append(silu_c)
                    def xproj_c():
                        xb = nb()
                        xref = (xb.ap(0, 96), xb.keys(0, 512))
                        for kj in range(16):
                            ctx.mm(xref, E.s(hb + kj * 128, 128), xpT.s(kj * 96, 96),
                                   start=(kj == 0), stop=(kj == 15))
                        ctx.cp("vector", xdbl.s(pp * 96, 96), xref)
                    chunks.append(xproj_c)
                    def trans_c():
                        tb = nb()
                        ctx.tr((tb.ap(0, 128, p0=0, pn=64), tb.keys(0, 512)),
                               xdbl.s(pp * 96, 64), ident.s())
                        ctx.cp("vector", dlrfm.s(pp * 128, 128, p0=0, pn=64),
                               (tb.ap(0, 128, p0=0, pn=64), tb.keys(0, 512)))
                        tb2 = nb()
                        ctx.tr((tb2.ap(0, 128, p0=0, pn=16), tb2.keys(0, 512)),
                               xdbl.s(pp * 96 + 64, 16), ident.s())
                        ctx.cp("vector", BCfm.s(pp * 256, 128, p0=0, pn=16),
                               (tb2.ap(0, 128, p0=0, pn=16), tb2.keys(0, 512)))
                        tb3 = nb()
                        ctx.tr((tb3.ap(0, 128, p0=0, pn=16), tb3.keys(0, 512)),
                               xdbl.s(pp * 96 + 80, 16), ident.s())
                        ctx.cp("vector", BCfm.s(pp * 256 + 128, 128, p0=0, pn=16),
                               (tb3.ap(0, 128, p0=0, pn=16), tb3.keys(0, 512)))
                    chunks.append(trans_c)
                    def dt4(g):
                        for dc in range(g * 4, g * 4 + 4):
                            db = nb()
                            dref = (db.ap(0, 128), db.keys(0, 512))
                            ctx.mm(dref, dtpT.s(dc * 128, 128), dlrfm.s(pp * 128, 128))
                            ctx.act(esc.s(pp * 128, 128), dref, AF.Exp, bias=dtb.s(dc, 1))
                            ctx.act(G.s(hb + dc * 128, 128), esc.s(pp * 128, 128), AF.Ln,
                                    bias=epsv.s(1, 1))
                    for g in range(4):
                        chunks.append(lambda g=g: dt4(g))
                    def bc4(g):
                        for n in range(g * 4, g * 4 + 4):
                            xb1 = nb()
                            ctx.mm((xb1.ap(0, 256, p0=0, pn=1), xb1.keys(0, 512)),
                                   ident.s(n, 1, p0=0, pn=16), BCfm.s(pp * 256, 256))
                            ctx.cp("vector", brow.s(pp * 256, 256),
                                   (xb1.ap(0, 256, p0=0, pn=1), xb1.keys(0, 512)))
                            bb = nb()
                            ctx.mm((bb.ap(0, 256), bb.keys(0, 512)),
                                   ones16.s(0, 128, p0=0, pn=1), brow.s(pp * 256, 256))
                            ctx.cp("scalar", C.s(hb + n * 128, 128),
                                   (bb.ap(0, 128), bb.keys(0, 512)))
                            ctx.op("scalar", "copy",
                                   [A.h[:, hb + n: hb + 2048: 16], bb.ap(128, 128)],
                                   {}, [bb.keys(0, 512)], [A.keys(hb, 2048)])
                    for g in range(4):
                        chunks.append(lambda g=g: bc4(g))
                    return chunks

            def m_back_chunks(mm_i):
                pp = mm_i % 2
                hb = pp * 2048
                ho = 2048 - hb

                def du_part(dc):
                    ctx.tt("gpsimd", convs.s(136 + (dc % 2) * 128, 128),
                           G.s(hb + dc * 128, 128), E.s(hb + dc * 128, 128), OP.mult)

                def dbu_half(dc, half):
                    slot = (dc * 2 + half) % 2
                    du_rep = bass.AP(convs.h, 136 + (dc % 2) * 128,
                                     [[convs.cols, 128], [0, 8], [1, 128]])
                    ctx.op("gpsimd", "tensor_tensor",
                           [H.h[:, ho + slot * 1024: ho + slot * 1024 + 1024]
                            .rearrange("p (a b) -> p a b", a=8),
                            du_rep,
                            C.h[:, hb + half * 1024: hb + half * 1024 + 1024]
                            .rearrange("p (a b) -> p a b", a=8),
                            OP.mult],
                           {}, [convs.keys(136 + (dc % 2) * 128, 128),
                                C.keys(hb + half * 1024, 1024)],
                           [H.keys(ho + slot * 1024, 1024)])

                def pre_c():
                    ctx.act(Fb.s(hb, 2048), Fb.s(hb, 2048), AF.Silu)
                    du_part(0)
                    dbu_half(0, 0)

                def dc_c(dc):
                    eb0 = hb + dc * 128
                    for half in range(2):
                        slot = (dc * 2 + half) % 2
                        for j in range(8):
                            n = half * 8 + j
                            ctx.act(dAb.s(j * 128, 128), G.s(eb0, 128), AF.Exp,
                                    scale=Asb.s(dc * 16 + n, 1))
                        for j in range(8):
                            n = half * 8 + j
                            ctx.op("vector", "tensor_tensor_scan",
                                   [H.h[:, hb + n: hb + 2048: 16], dAb.ap(j * 128, 128),
                                    H.h[:, ho + slot * 1024 + j * 128:
                                         ho + slot * 1024 + (j + 1) * 128],
                                    state.ap(dc * 16 + n, 1), OP.mult, OP.add], {},
                                   [dAb.keys(j * 128, 128),
                                    H.keys(ho + slot * 1024 + j * 128, 128)],
                                   [H.keys(hb, 2048)],
                                   ptr_reads=[state.keys(dc * 16 + n, 1)],
                                   late=[H.keys(hb, 2048)])
                        if half == 0:
                            dbu_half(dc, 1)
                        elif dc + 1 < 16:
                            du_part(dc + 1)
                            dbu_half(dc + 1, 0)
                    ctx.cp("vector", state.s(dc * 16, 16), H.s(hb + 2032, 16))
                    ctx.tt("vector", H.s(hb, 2048), H.s(hb, 2048), A.s(hb, 2048), OP.mult)
                    h3 = H.h[:, hb: hb + 2048].rearrange("p (t n) -> p t n", n=16)
                    ctx.op("vector", "tensor_reduce", [yv.ap(0, 128), h3],
                           dict(axis=AX.X, op=OP.add), [H.keys(hb, 2048)],
                           [yv.keys(0, 128)], late=[yv.keys(0, 128)])
                    ctx.stt("vector", yv.s(0, 128), E.s(eb0, 128),
                            dvec.s(dc, 1), yv.s(0, 128), OP.mult, OP.add)
                    ctx.tt("vector", E.s(eb0, 128), yv.s(0, 128),
                           Fb.s(eb0, 128), OP.mult)

                def out_c():
                    banks2 = [nb() for _ in range(2)]
                    for kj in range(16):
                        wch = fetch(opT_d, kj * 128, 0, 1024)
                        for ob in range(2):
                            rhs = (wch[0][:, ob * 512:(ob + 1) * 512], wch[1])
                            ctx.mm((banks2[ob].ap(), banks2[ob].keys(0, 512)),
                                   E.s(hb + kj * 128, 128), rhs,
                                   start=(kj == 0), stop=(kj == 15))
                    for ob in range(2):
                        col = mm_i * 1024 + ob * 512
                        ctx.tt("vector", B.s(col, 512),
                               (banks2[ob].ap(), banks2[ob].keys(0, 512)),
                               B.s(col, 512), OP.add)

                return pre_c, [lambda dc=dc: dc_c(dc) for dc in range(16)], out_c

            def ffn2_weave_chunks():
                ch = [lambda: layer_norm(B, 0, 0, scr=Fb.s(0, 1024)),
                      lambda: layer_norm(B, 1, 4, scr=Fb.s(0, 1024)),
                      lambda: ffn_tr(range(0, 4), [0, 1]),
                      lambda: ffn_tr(range(4, 8), [0, 1]),
                      lambda: ffn_gate(0),
                      lambda: ffn_gate(1)]
                for ex in range(2):
                    for jb in range(2):
                        ch.append(lambda ex=ex, jb=jb: ffn_expert(ex, jb, [0, 1]))
                return ch

            def ffn2_tail():
                layer_norm(B, 2, 8)
                layer_norm(B, 3, 12)
                ffn_tr(range(8), [2, 3])
                ffn_gate(2)
                ffn_gate(3)
                for ex in range(2):
                    for jb in range(2):
                        ffn_expert(ex, jb, [2, 3])

            for k in range(5):
                if k < 4:
                    F = m_front_chunks(k)
                elif N_SUBLAYERS >= 4:
                    F = ffn2_weave_chunks()
                else:
                    F = []
                if k == 0:
                    for f in F:
                        f()
                    continue
                pre, dcs, out = m_back_chunks(k - 1)
                pre()
                fi = 0
                for i, b in enumerate(dcs):
                    b()
                    want = ((i + 1) * len(F) + 15) // 16
                    while fi < want:
                        F[fi]()
                        fi += 1
                while fi < len(F):
                    F[fi]()
                    fi += 1
                out()
        if N_SUBLAYERS >= 4:
            ffn2_tail()

        for t_s in range(4):
            ctx.store(out_d[T * 512 + t_s * 128: T * 512 + (t_s + 1) * 128, :],
                      B.s(t_s * 1024, 1024))
    if DBG:
        ctx.barrier()
        ctx.store(dbg_d["dKVZ"][:, 0:KVZ.cols], KVZ.s())
        ctx.store(dbg_d["dC"][:, :], C.s())
        ctx.store(dbg_d["dE"][:, :], E.s())
        ctx.store(dbg_d["dF"][:, :], Fb.s())
        ctx.store(dbg_d["dD"][:, :], Dd.s())
        ctx.store(dbg_d["dkp"][:, 0:512], kp.s())
        ctx.store(dbg_d["dvtm"][:, 0:512], vtm.s())
        ctx.store(dbg_d["dkptm"][:, 0:512], kptm.s())
        ctx.store(dbg_d["dKVT"][:, 0:2048], G.s(0, 2048))
        ctx.store(dbg_d["dKsum"][:, 0:16], Ksum.s())

    ctx.emit()
    return nc



_CACHE = {}

def host_shared(L0_attn_a_Wq, L0_attn_a_Wk, L0_attn_a_Wv, L0_attn_a_Wo,
                L0_attn_a_rf, L0_attn_b_norm_w, L0_attn_b_in_proj_w, L0_attn_b_conv_w,
                L0_attn_b_conv_b, L0_attn_b_x_proj_w, L0_attn_b_dt_proj_w,
                L0_attn_b_dt_proj_b, L0_attn_b_A_log, L0_attn_b_D, L0_attn_b_out_proj_w,
                L0_ffn_gate_w, L0_ffn_w0, L0_ffn_b0, L0_ffn_w1, L0_ffn_b1, **_ignore):
    f = np.float32
    cont = np.ascontiguousarray

    shp_v = np.zeros((128, 1), f); shp_v[0:64] = np.pi / 2
    pi = np.pi
    shared = {
        "wq": cont(L0_attn_a_Wq, dtype=f), "wk": cont(L0_attn_a_Wk, dtype=f),
        "wv": cont(L0_attn_a_Wv, dtype=f), "wo": cont(L0_attn_a_Wo, dtype=f),
        "rf2": cont(np.concatenate([L0_attn_a_rf, L0_attn_a_rf], axis=0), dtype=f),
        "gw": cont(L0_ffn_gate_w, dtype=f),
        "w0T": cont(L0_ffn_w0.T, dtype=f), "w1T": cont(L0_ffn_w1.T, dtype=f),
        "b0r": cont(L0_ffn_b0.reshape(1, DM), dtype=f),
        "b1r": cont(L0_ffn_b1.reshape(1, DM), dtype=f),
        "normw": cont(L0_attn_b_norm_w.reshape(8, 128).T, dtype=f),
        "inpT": cont(L0_attn_b_in_proj_w.T, dtype=f),
        "convw": cont(L0_attn_b_conv_w[:, 0, :].reshape(16, 128, 4)
                      .transpose(1, 0, 2).reshape(128, 64), dtype=f),
        "convb": cont(L0_attn_b_conv_b.reshape(16, 128).T, dtype=f),
        "xpT": cont(L0_attn_b_x_proj_w.T, dtype=f),
        "dtpT": cont(L0_attn_b_dt_proj_w.T, dtype=f),
        "dtb": cont(L0_attn_b_dt_proj_b.reshape(16, 128).T, dtype=f),
        "alog": cont(np.asarray(L0_attn_b_A_log).reshape(16, 128, 16)
                     .transpose(1, 0, 2).reshape(128, 256), dtype=f),
        "dvec": cont(L0_attn_b_D.reshape(16, 128).T, dtype=f),
        "opT": cont(L0_attn_b_out_proj_w.T, dtype=f),
        "ident": np.eye(128, dtype=f),
        "ones16": np.ones((16, 128), f),
        "shp": shp_v,
        "b1v": cont(shp_v - pi, dtype=f),
        "b2v": cont(-shp_v - pi, dtype=f),
        "epsv": cont(np.broadcast_to(np.array([1e-5, 1.0], f), (128, 2)), dtype=f),
    }
    return shared


def kernel(L0_attn_a_x, **kw):
    shared = host_shared(**kw)
    X = np.ascontiguousarray(L0_attn_a_x, dtype=np.float32)
    in_maps = [dict(shared, x=np.ascontiguousarray(X[b])) for b in range(NCORES)]

    if "nc" not in _CACHE:
        _CACHE["nc"] = build_program()
    nc = _CACHE["nc"]
    res = run_bass_kernel_spmd(nc, in_maps, core_ids=list(range(NCORES)))
    out = np.stack([np.asarray(res.results[b]["out"]) for b in range(NCORES)], axis=0)
    return out.astype(np.float32)
